# version 02
# baseline (speedup 1.0000x reference)
import numpy as np
import sys
if '/opt/trn_rl_repo' not in sys.path:
    sys.path.insert(0, '/opt/trn_rl_repo')
from contextlib import ExitStack
import concourse.bass as bass
import concourse.mybir as mybir

F32 = mybir.dt.float32
BF16 = mybir.dt.bfloat16
U8 = mybir.dt.uint8
AF = mybir.ActivationFunctionType
ALU = mybir.AluOpType
AX = mybir.AxisListType.X

B, S, HID = 4, 2048, 4096
II = 11008
T = 1024
BT = 256
TE = BT + T
XPAD = 1408
CH = 128
NCH = TE // CH
NPAIR = 32
FC = HID // 128
IC = II // 128
EPS = 1e-5
CLAMP = 41.0

SEMS = ("gd", "xs", "td0", "td1", "ss0", "ss1", "wd0", "wd1", "ps", "gs", "as6",
        "od", "ve", "ae", "pe", "ge",
        "cdk", "cdv", "cdr", "cdg", "cdo", "cdfg", "cdfu", "cdfd")


class Sem:
    def __init__(self, h):
        self.h = h
        self.n = 0
    def inc(self, k=1):
        self.n += k
        return self.n


def build_program(debug_stop=None):
    nc = bass.Bass("TRN2", target_bir_lowering=False)

    x_in = nc.dram_tensor("x", [XPAD, HID], F32, kind="ExternalInput").ap()
    mask_in = nc.dram_tensor("mask", [128, 1], F32, kind="ExternalInput").ap()
    wT = {m: nc.dram_tensor(f"w{m}T", [HID, HID], F32, kind="ExternalInput").ap()
          for m in ("k", "v", "r", "g", "o")}
    mx_in = {m: nc.dram_tensor(f"mx{m}", [128, FC], F32, kind="ExternalInput").ap()
             for m in ("k", "v", "r", "g")}
    tdec = nc.dram_tensor("tdec", [128, FC], F32, kind="ExternalInput").ap()
    tdecr = nc.dram_tensor("tdecr", [1, HID], F32, kind="ExternalInput").ap()
    tfaa = nc.dram_tensor("tfaa", [128, FC], F32, kind="ExternalInput").ap()
    gnw_in = nc.dram_tensor("gnw", [1, HID], F32, kind="ExternalInput").ap()
    gnb_in = nc.dram_tensor("gnb", [1, HID], F32, kind="ExternalInput").ap()
    wgT2 = nc.dram_tensor("wgT2", [HID, II], F32, kind="ExternalInput").ap()
    wuT2 = nc.dram_tensor("wuT2", [HID, II], F32, kind="ExternalInput").ap()
    wdT2 = nc.dram_tensor("wdT2", [II, HID], F32, kind="ExternalInput").ap()
    y_out = nc.dram_tensor("y", [T, HID], F32, kind="ExternalOutput").ap()

    xn_tm = nc.dram_tensor("xn_tm", [XPAD, HID], BF16).ap()
    kT_d = nc.dram_tensor("kT_d", [HID, TE], BF16).ap()
    vT_d = nc.dram_tensor("vT_d", [HID, TE], BF16).ap()
    rT_d = nc.dram_tensor("rT_d", [HID, T], BF16).ap()
    gT_d = nc.dram_tensor("gT_d", [HID, T], BF16).ap()
    gated_tm = nc.dram_tensor("gated_tm", [T, HID], BF16).ap()
    attnd_tm = nc.dram_tensor("attnd_tm", [T, HID], BF16).ap()
    wb = {m: nc.dram_tensor(f"wb{m}", [HID, HID], BF16).ap() for m in ("k", "v", "r", "g", "o")}
    wfb = {"g": nc.dram_tensor("wgb2", [HID, II], BF16).ap(),
           "u": nc.dram_tensor("wub2", [HID, II], BF16).ap(),
           "d": nc.dram_tensor("wdb2", [II, HID], BF16).ap()}

    dbg = {}
    if debug_stop == "ln":
        dbg["xn"] = nc.dram_tensor("dbg_xn", [XPAD, HID], F32, kind="ExternalOutput").ap()
    if debug_stop == "proj":
        for nmm, ln_ in (("kT", TE), ("vT", TE), ("rT", T), ("gT", T)):
            dbg[nmm] = nc.dram_tensor("dbg_" + nmm, [HID, ln_], F32, kind="ExternalOutput").ap()
    if debug_stop == "scan":
        dbg["gated"] = nc.dram_tensor("dbg_gated", [T, HID], F32, kind="ExternalOutput").ap()
    if debug_stop == "attnd":
        dbg["attnd"] = nc.dram_tensor("dbg_attnd", [T, HID], F32, kind="ExternalOutput").ap()

    es = ExitStack()
    with es:
        sbt = lambda n_, s_, d_: es.enter_context(nc.sbuf_tensor(n_, s_, d_))
        pst = lambda n_, s_: es.enter_context(nc.psum_tensor(n_, s_, F32))
        c_mask = sbt("c_mask", [128, 1], F32)
        c_mx = sbt("c_mx", [128, 4 * FC], F32)
        c_L = sbt("c_L", [128, FC], F32)
        c_u = sbt("c_u", [128, FC], F32)
        c_ones = sbt("c_ones", [128, 1], BF16)
        c_eps = sbt("c_eps", [128, 1], F32)
        c_onesrow = sbt("c_onesrow", [1, 128], F32)
        c_zero = sbt("c_zero", [128, 128], BF16)
        c_selmask = sbt("c_selmask", [128, 128], U8)
        c_pat = sbt("c_pat", [128, 1536], F32)
        A1 = sbt("A1", [128, 49152], BF16)
        A2 = sbt("A2", [128, 24576], BF16)
        A3 = sbt("A3", [128, 8192], F32)
        pA = pst("pA", [128, 1536])
        pB = pst("pB", [128, 1536])
        pC = pst("pC", [128, 1024])
        _s = {k: es.enter_context(nc.semaphore("s_" + k)) for k in SEMS}
        block = es.enter_context(nc.Block())

        sm = {k: Sem(_s[k]) for k in SEMS}
        gd, xs, ps, gs, as6, od = sm["gd"], sm["xs"], sm["ps"], sm["gs"], sm["as6"], sm["od"]
        ve, ae, pe, ge = sm["ve"], sm["ae"], sm["pe"], sm["ge"]
        tdp = (sm["td0"], sm["td1"])
        ssp = (sm["ss0"], sm["ss1"])
        wdp = (sm["wd0"], sm["wd1"])
        allsems = tuple(sm.values())

        plan = {e: [] for e in ("sync", "scalar", "vector", "tensor", "gpsimd")}

        def op(eng, fn, waits=(), inc=None, dma=False):
            waits = [(w[0], w[1]) for w in waits]
            if not dma:
                if eng == "vector":
                    assert inc is None or inc[0] is ve
                    waits.append((ve, ve.n))
                    inc = (ve, 1)
                elif eng == "scalar":
                    assert inc is None or inc[0] is ae
                    waits.append((ae, ae.n))
                    inc = (ae, 1)
                elif eng == "gpsimd" and inc is not None and inc[0] is ge:
                    waits.append((ge, ge.n))
            if inc is not None:
                inc[0].inc(inc[1])
            plan[eng].append((fn, waits, inc))

        def W(*sems_):
            return [(s_, s_.n) for s_ in sems_]

        op("gpsimd", lambda e: e.dma_start(out=c_mask[:, :], in_=mask_in), inc=(gd, 16))
        for i, m in enumerate(("k", "v", "r", "g")):
            op("gpsimd", lambda e, i=i, m=m: e.dma_start(out=c_mx[:, i * FC:(i + 1) * FC], in_=mx_in[m]), inc=(gd, 16))
        op("gpsimd", lambda e: e.dma_start(out=c_L[:, :], in_=tdec), inc=(gd, 16))
        op("gpsimd", lambda e: e.dma_start(out=c_u[:, :], in_=tfaa), inc=(gd, 16))
        cl = W(gd)
        op("scalar", lambda e: e.activation(c_L[:, :], c_L[:, :], AF.Exp), waits=cl, inc=(ae, 1))
        op("vector", lambda e: e.tensor_scalar_mul(c_L[:, :], c_L[:, :], -1.0), waits=W(ae), inc=(ve, 1))
        op("vector", lambda e: e.memset(c_onesrow[0:1, :], 1.0), inc=(ve, 1))
        op("vector", lambda e: e.memset(c_ones[:, :], 1.0), inc=(ve, 1))
        op("vector", lambda e: e.memset(c_eps[:, :], EPS), inc=(ve, 1))
        op("vector", lambda e: e.memset(c_zero[:, :], 0.0), inc=(ve, 1))
        i_scr = c_pat[:, 1408:1536]
        op("gpsimd", lambda e: e.iota(i_scr, pattern=[[1, 128]], base=0, channel_multiplier=-1,
                                      allow_small_or_imprecise_dtypes=True), inc=(ge, 1))
        op("vector", lambda e: e.tensor_single_scalar(c_selmask[:, :], i_scr, 0.0, ALU.is_gt),
           waits=W(ge), inc=(ve, 1))
        pats = {
            0: ([[0, 8], [1, 16]], -8, 0),
            1: ([[0, 8], [-1, 16]], 7, 0),
            3: ([[0, 4], [-1, 32]], 15, 0),
            4: ([[0, 4], [1, 32]], -16, 0),
            5: ([[0, 2], [-1, 64]], 31, 0),
            6: ([[0, 2], [1, 64]], -32, 0),
            7: ([[3000, 2], [-1, 64]], 63, 0),
            8: ([[-3000, 2], [1, 64]], 3000, 0),
            9: ([[1, 128]], 0, 0),
            10: ([[0, 128]], 127, -1),
        }
        def pat(s_):
            return c_pat[:, s_ * 128:(s_ + 1) * 128]
        for s_, (pp_, bb_, cm_) in pats.items():
            op("gpsimd", lambda e, s_=s_, pp_=pp_, bb_=bb_, cm_=cm_:
               e.iota(pat(s_), pattern=pp_, base=bb_, channel_multiplier=cm_,
                      allow_small_or_imprecise_dtypes=True), inc=(ge, 1))
        op("gpsimd", lambda e: e.iota(pat(2), pattern=[[0, 4], [-3000, 2], [0, 16]], base=3000,
                                      channel_multiplier=0, allow_small_or_imprecise_dtypes=True), inc=(ge, 1))
        op("vector", lambda e: e.tensor_add(pat(2), pat(2), pat(1)), waits=W(ge), inc=(ve, 1))
        ph0 = W(ve, ge, ae, gd)

        cast_marks = {}
        jobs = [("cdk", wT["k"], wb["k"], HID), ("cdv", wT["v"], wb["v"], HID),
                ("cdr", wT["r"], wb["r"], HID), ("cdg", wT["g"], wb["g"], HID),
                ("cdo", wT["o"], wb["o"], HID),
                ("cdfg", wgT2, wfb["g"], HID), ("cdfu", wuT2, wfb["u"], HID),
                ("cdfd", wdT2, wfb["d"], II)]
        for snm, src_, dst_, R_ in jobs:
            for r0 in range(0, R_, 2048):
                r1 = min(r0 + 2048, R_)
                op("gpsimd", lambda e, src_=src_, dst_=dst_, r0=r0, r1=r1:
                   e.dma_start(out=dst_[r0:r1, :], in_=src_[r0:r1, :]), inc=(sm[snm], 16))
            cast_marks[snm] = W(sm[snm])

        xa = A3.ap()[:, 0:4096]
        xsq = A3.ap()[:, 4096:8192]
        st_mu = c_pat[:, 1408:1409]
        st_s2 = c_pat[:, 1409:1410]
        st_rs = c_pat[:, 1410:1411]
        st_t1 = c_pat[:, 1411:1412]
        xnst = A2.ap()[:, 0:4096]
        for it in range(XPAD // 128):
            lw = list(ph0) if it == 0 else W(ve, ae)
            op("gpsimd", lambda e, it=it: e.dma_start(out=xa, in_=x_in[it * 128:(it + 1) * 128, :]),
               waits=lw, inc=(gd, 16))
            ld = W(gd)
            op("vector", lambda e: e.reduce_sum(st_mu, xa, axis=AX), waits=ld, inc=(ve, 1))
            op("scalar", lambda e: e.activation(xsq, xa, AF.Square), waits=ld, inc=(ae, 1))
            op("vector", lambda e: e.reduce_sum(st_s2, xsq, axis=AX), waits=W(ae), inc=(ve, 1))
            op("vector", lambda e: e.tensor_scalar_mul(st_mu, st_mu, 1.0 / HID), inc=(ve, 1))
            op("vector", lambda e: e.tensor_scalar_mul(st_s2, st_s2, 1.0 / HID), inc=(ve, 1))
            op("vector", lambda e: e.tensor_tensor(st_t1, st_mu, st_mu, op=ALU.mult), inc=(ve, 1))
            op("vector", lambda e: e.tensor_sub(st_s2, st_s2, st_t1), inc=(ve, 1))
            op("scalar", lambda e: e.activation(st_s2, st_s2, AF.Sqrt, bias=c_eps[:, 0:1]), waits=W(ve), inc=(ae, 1))
            op("vector", lambda e: e.reciprocal(st_rs, st_s2), waits=W(ae), inc=(ve, 1))
            op("vector", lambda e: e.tensor_scalar(xnst, xa, st_mu, st_rs, ALU.subtract, ALU.mult),
               waits=W(xs), inc=(ve, 1))
            op("sync", lambda e, it=it: e.dma_start(out=xn_tm[it * 128:(it + 1) * 128, :], in_=xnst),
               waits=W(ve), inc=(xs, 16))
        op("gpsimd", lambda e: e.dma_start(out=xnst[0:1, :], in_=xn_tm[256:257, :]), waits=W(xs), inc=(gd, 16))
        op("vector", lambda e: e.tensor_scalar_mul(xnst[0:1, :], xnst[0:1, :], c_mask[0:1, 0:1]),
           waits=W(gd), inc=(ve, 1))
        op("sync", lambda e: e.dma_start(out=xn_tm[256:257, :], in_=xnst[0:1, :]), waits=W(ve), inc=(xs, 16))
        xn_ready = W(xs)

        if debug_stop == "ln":
            for it in range(XPAD // 128):
                op("gpsimd", lambda e, it=it: e.dma_start(out=dbg["xn"][it * 128:(it + 1) * 128, :],
                                                          in_=xn_tm[it * 128:(it + 1) * 128, :]),
                   waits=xn_ready, inc=(gd, 16))
            _emit(block, plan, allsems)
            return nc

        mixT = A1.ap()[:, 0:FC * TE]
        def mix_sl(j, t0, tl):
            return mixT[:, j * TE + t0: j * TE + t0 + tl]
        xnc = [A2.ap()[:, 0:1408], A2.ap()[:, 1408:2816]]
        delta = A2.ap()[:, 2816:4224]
        slab = [A2.ap()[:, 4224 + i * 512:4224 + (i + 1) * 512] for i in range(2)]
        evst = A2.ap()[:, 5248:5248 + 1280]
        mix_idx = {"k": 0, "v": 1, "r": 2, "g": 3}
        projs = [("k", TE, 0, kT_d, "cdk"), ("v", TE, 0, vT_d, "cdv"),
                 ("r", T, BT, rT_d, "cdr"), ("g", T, BT, gT_d, "cdg")]
        for pi, (m, TL, toff, dst_d, csem) in enumerate(projs):
            for j in range(FC):
                bsl = xnc[j % 2]
                tsem = tdp[j % 2]
                ww = (xn_ready if (pi == 0 and j < 2) else W(ve))
                op("scalar", lambda e, j=j, bsl=bsl: e.dma_start_transpose(bsl, xn_tm[:, j * 128:(j + 1) * 128]),
                   waits=ww, inc=(tsem, 16), dma=True)
                op("vector", lambda e, bsl=bsl, toff=toff, TL=TL:
                   e.tensor_sub(delta[:, 0:TL], bsl[:, 1 + toff:1 + toff + TL], bsl[:, toff:toff + TL]),
                   waits=W(tsem), inc=(ve, 1))
                op("vector", lambda e, j=j, m=m, bsl=bsl, toff=toff, TL=TL:
                   e.scalar_tensor_tensor(mix_sl(j, 0, TL), delta[:, 0:TL],
                                          c_mx[:, mix_idx[m] * FC + j: mix_idx[m] * FC + j + 1],
                                          bsl[:, toff:toff + TL], ALU.mult, ALU.add), inc=(ve, 1))
            for imac in range(HID // 256):
                i0 = imac * 256
                for kk in range(FC):
                    bsl = slab[(kk // 2) % 2]
                    wsem = wdp[(kk // 2) % 2]
                    if kk % 2 == 0:
                        ww = (cast_marks[csem] + W(ve)) if (imac == 0 and kk == 0) else W(pe)
                        op("sync", lambda e, m=m, kk=kk, i0=i0, bsl=bsl:
                           e.dma_start(out=bsl[:, 0:512].rearrange("p (a i) -> p a i", a=2),
                                       in_=wb[m][kk * 128:(kk + 2) * 128, i0:i0 + 256]
                                       .rearrange("(a p) i -> p a i", p=128)),
                           waits=ww, inc=(wsem, 16))
                    for q in range(2):
                        pt = (pA, pB)[q]
                        for nn in range((TL + 511) // 512):
                            n0, n1 = nn * 512, min((nn + 1) * 512, TL)
                            op("tensor", lambda e, kk=kk, q=q, pt=pt, n0=n0, n1=n1, bsl=bsl:
                               e.matmul(pt[:, n0:n1],
                                        bsl[:, (kk % 2) * 256 + q * 128:(kk % 2) * 256 + (q + 1) * 128],
                                        mix_sl(kk, n0, n1 - n0),
                                        start=(kk == 0), stop=(kk == FC - 1)),
                               waits=W(wsem) + (W(ve) if kk == 0 else []), inc=(pe, 1))
                for q in range(2):
                    pt = (pA, pB)[q]
                    op("vector", lambda e, pt=pt, TL=TL: e.tensor_copy(evst[:, 0:TL], pt[:, 0:TL]),
                       waits=W(pe, ps), inc=(ve, 1))
                    op("sync", lambda e, i0=i0, q=q, dst_d=dst_d, TL=TL:
                       e.dma_start(out=dst_d[i0 + q * 128:i0 + (q + 1) * 128, 0:TL], in_=evst[:, 0:TL]),
                       waits=W(ve), inc=(ps, 16))
        proj_done = W(ps)

        if debug_stop == "proj":
            for nmm, src_d, TL in (("kT", kT_d, TE), ("vT", vT_d, TE), ("rT", rT_d, T), ("gT", gT_d, T)):
                for it in range(FC):
                    op("gpsimd", lambda e, nmm=nmm, src_d=src_d, it=it, TL=TL:
                       e.dma_start(out=dbg[nmm][it * 128:(it + 1) * 128, :], in_=src_d[it * 128:(it + 1) * 128, :]),
                       waits=proj_done, inc=(gd, 16))
            _emit(block, plan, allsems)
            return nc

        Lrow = A3.ap()[:, 0:4096]
        xrow = A3.ap()[0:1, 4096:8192]
        Sst = A3.ap()[0:64, 4224:4352]
        escr = A3.ap()[:, 4352:4480]
        GNS = A3.ap()[:, 4480:4608]
        gsb = A1.ap()[:, 0:32768]
        gnw_t = A1.ap()[:, 32768:36864]
        gnb_t = A1.ap()[:, 36864:40960]
        pwp = [A1.ap()[:, 40960 + i * 128: 40960 + (i + 1) * 128] for i in range(12)]
        def a2v(i_):
            return A2.ap()[:, i_ * 128:(i_ + 1) * 128]
        rfb = [a2v(0), a2v(1)]
        kfb = [a2v(2), a2v(3)]
        ktb = [a2v(4), a2v(5)]
        vtb = [a2v(6), a2v(7)]
        gtb = [a2v(8), a2v(9)]
        stt = [[a2v(10 + p * 14 + i) for i in range(14)] for p in range(2)]

        op("gpsimd", lambda e: e.dma_start(out=xrow[0:1, 0:HID], in_=tdecr), waits=proj_done, inc=(gd, 16))
        op("scalar", lambda e: e.activation(xrow[0:1, 0:HID], xrow[0:1, 0:HID], AF.Exp), waits=W(gd), inc=(ae, 1))
        op("vector", lambda e: e.tensor_scalar_mul(xrow[0:1, 0:HID], xrow[0:1, 0:HID], -1.0), waits=W(ae), inc=(ve, 1))
        for q in range(8):
            op("tensor", lambda e, q=q: e.matmul(pC[:, 0:512], c_onesrow[0:1, :], xrow[0:1, q * 512:(q + 1) * 512],
                                                 start=True, stop=True), waits=W(ve), inc=(pe, 1))
            op("vector", lambda e, q=q: e.tensor_copy(Lrow[:, q * 512:(q + 1) * 512], pC[:, 0:512]),
               waits=W(pe), inc=(ve, 1))
        for (dst_t, src_r) in ((gnw_t, gnw_in), (gnb_t, gnb_in)):
            op("gpsimd", lambda e, src_r=src_r: e.dma_start(out=xrow[0:1, 0:HID], in_=src_r), waits=W(ve), inc=(gd, 16))
            for q in range(8):
                op("tensor", lambda e, q=q: e.matmul(pC[:, 0:512], c_onesrow[0:1, :], xrow[0:1, q * 512:(q + 1) * 512],
                                                     start=True, stop=True), waits=W(gd, ve), inc=(pe, 1))
                op("vector", lambda e, dst_t=dst_t, q=q: e.tensor_copy(dst_t[:, q * 512:(q + 1) * 512], pC[:, 0:512]),
                   waits=W(pe), inc=(ve, 1))
        scan_pre = W(ve, pe, ae, gd)

        PS = {"AA": pA[:, 0:128],
              "AB": pA[:, 512:640],
              "OA": pA[:, 1024:1088],
              "OB": pB[:, 0:64],
              "cA": pB[:, 512:513],
              "cB": pB[:, 1024:1025],
              "SA": pC[0:64, 0:64],
              "SB": pC[0:64, 512:576]}

        for mpr in range(NPAIR):
            Lcol = c_L[:, mpr:mpr + 1]
            ucol = c_u[:, mpr:mpr + 1]
            first_w = scan_pre if mpr == 0 else []
            def mkpow(dst_t, pslot, extra=()):
                op("vector", lambda e, pslot=pslot, Lcol=Lcol:
                   e.tensor_scalar(escr, pat(pslot), Lcol, CLAMP, ALU.mult, ALU.min),
                   waits=list(extra) + W(ae, ve), inc=(ve, 1))
                op("scalar", lambda e, dst_t=dst_t: e.activation(dst_t, escr, AF.Exp),
                   waits=W(ve), inc=(ae, 1))
            for slot, pslot in ((0, 0), (1, 1), (2, 2), (3, 3), (4, 4), (5, 5), (6, 6), (7, 7), (8, 8), (9, 9)):
                mkpow(pwp[slot], pslot, extra=first_w if slot == 0 else ())
            op("vector", lambda e, mpr=mpr:
               e.tensor_tensor(escr, pat(10), Lrow[:, mpr * 128:(mpr + 1) * 128], op=ALU.mult),
               waits=W(ae, ve), inc=(ve, 1))
            op("scalar", lambda e: e.activation(pwp[10], escr, AF.Exp), waits=W(ve), inc=(ae, 1))
            op("vector", lambda e, Lcol=Lcol: e.tensor_single_scalar(GNS[:, 8:9], Lcol, 128.0, ALU.mult),
               waits=W(ae), inc=(ve, 1))
            op("scalar", lambda e: e.activation(GNS[:, 8:9], GNS[:, 8:9], AF.Exp), waits=W(ve), inc=(ae, 1))
            op("vector", lambda e: e.tensor_copy(GNS[0:64, 9:10], GNS[64:128, 8:9]), waits=W(ae), inc=(ve, 1))
            pows = W(ae, ve)
            op("vector", lambda e: e.memset(Sst[:, :], 0.0), waits=W(pe), inc=(ve, 1))

            for n in range(NCH):
                cpar = n % 2
                boot = n < 2
                st = stt[cpar]
                rf, kf, kt, vt, gt = rfb[cpar], kfb[cpar], ktb[cpar], vtb[cpar], gtb[cpar]
                tsem, ssem = tdp[cpar], ssp[cpar]
                t0e = n * CH
                ldw = (pows if n < 2 else W(pe, ve))
                op("scalar", lambda e, mpr=mpr, t0e=t0e, kt=kt:
                   e.dma_start_transpose(kt, kT_d[mpr * 128:(mpr + 1) * 128, t0e:t0e + CH]), waits=ldw, inc=(tsem, 16), dma=True)
                op("scalar", lambda e, mpr=mpr, t0e=t0e, vt=vt:
                   e.dma_start_transpose(vt, vT_d[mpr * 128:(mpr + 1) * 128, t0e:t0e + CH]), inc=(tsem, 16), dma=True)
                op("sync", lambda e, mpr=mpr, t0e=t0e, kf=kf:
                   e.dma_start(out=kf, in_=kT_d[mpr * 128:(mpr + 1) * 128, t0e:t0e + CH]), waits=ldw, inc=(ssem, 16))
                if not boot:
                    top = t0e - BT
                    op("scalar", lambda e, mpr=mpr, top=top, gt=gt:
                       e.dma_start_transpose(gt, gT_d[mpr * 128:(mpr + 1) * 128, top:top + CH]), inc=(tsem, 16), dma=True)
                    op("sync", lambda e, mpr=mpr, top=top, rf=rf:
                       e.dma_start(out=rf, in_=rT_d[mpr * 128:(mpr + 1) * 128, top:top + CH]), inc=(ssem, 16))
                ch_ld = W(tsem, ssem)
                if boot:
                    op("vector", lambda e, kt=kt: e.tensor_scalar_mul(kt, kt, c_mask[:, 0:1]),
                       waits=ch_ld, inc=(ve, 1))
                    ch_ld = W(tsem, ssem, ve)
                else:
                    op("scalar", lambda e, gt=gt: e.activation(gt, gt, AF.Silu), waits=ch_ld, inc=(ae, 1))
                op("vector", lambda e, kt=kt, st=st: e.tensor_tensor(st[0], kt, pwp[10], op=ALU.mult),
                   waits=ch_ld, inc=(ve, 1))
                if not boot:
                    op("vector", lambda e, rf=rf, st=st: e.tensor_tensor(st[1], rf, pwp[0], op=ALU.mult), inc=(ve, 1))
                    op("vector", lambda e, kf=kf, st=st: e.tensor_tensor(st[2], kf, pwp[1], op=ALU.mult), inc=(ve, 1))
                    op("vector", lambda e, kf=kf, st=st: e.tensor_tensor(st[3], kf, pwp[2], op=ALU.mult), inc=(ve, 1))
                    op("vector", lambda e, kf=kf, st=st: e.tensor_tensor(st[4], kf, pwp[3], op=ALU.mult), inc=(ve, 1))
                    op("vector", lambda e, rf=rf, st=st: e.tensor_tensor(st[5], rf, pwp[4], op=ALU.mult), inc=(ve, 1))
                    op("vector", lambda e, kf=kf, st=st: e.tensor_tensor(st[6], kf, pwp[5], op=ALU.mult), inc=(ve, 1))
                    op("vector", lambda e, rf=rf, st=st: e.tensor_tensor(st[7], rf, pwp[6], op=ALU.mult), inc=(ve, 1))
                    op("vector", lambda e, kf=kf, st=st: e.tensor_tensor(st[8], kf, pwp[7], op=ALU.mult), inc=(ve, 1))
                    op("vector", lambda e, rf=rf, st=st: e.tensor_tensor(st[9], rf, pwp[8], op=ALU.mult), inc=(ve, 1))
                    op("vector", lambda e, rf=rf, st=st: e.tensor_tensor(st[10][0:64, :], rf[0:64, :], pwp[9][0:64, :], op=ALU.mult), inc=(ve, 1))
                    op("vector", lambda e, rf=rf, st=st: e.tensor_tensor(st[11][0:64, :], rf[64:128, :], pwp[9][64:128, :], op=ALU.mult), inc=(ve, 1))
                    op("vector", lambda e, rf=rf, st=st, ucol=ucol: e.tensor_scalar_mul(st[12], rf, ucol), inc=(ve, 1))
                    op("vector", lambda e, kf=kf, st=st: e.tensor_tensor(st[12], st[12], kf, op=ALU.mult), inc=(ve, 1))
                scaled = W(ve)

                if not boot:
                    for hh in range(2):
                        dps = PS["AA"] if hh == 0 else PS["AB"]
                        rb = slice(hh * 64, hh * 64 + 64)
                        tp = hh * 64
                        op("tensor", lambda e, dps=dps, rb=rb, tp=tp, st=st:
                           e.matmul(dps[0:128, 0:128], st[8][rb, :], st[9][rb, :],
                                    start=True, stop=True, tile_position=(tp, 0),
                                    skip_group_check=True),
                           waits=(scaled if hh == 0 else ()), inc=(pe, 1))
                        def AMM(lhs, l0, l1, r0_, r1_, o0, o1, t0_, t1_, waits_=()):
                            op("tensor", lambda e, lhs=lhs, l0=l0, l1=l1, r0_=r0_, r1_=r1_, o0=o0, o1=o1,
                               t0_=t0_, t1_=t1_, dps=dps, rb=rb, tp=tp, st=st:
                               e.matmul(dps[o0:o1, t0_:t1_], lhs[rb, l0:l1], st[1][rb, r0_:r1_],
                                        start=False, stop=True, tile_position=(tp, o0 - o0 % 32),
                                        skip_group_check=True),
                               waits=list(waits_), inc=(pe, 1))
                        for kb in range(4):
                            b0 = kb * 32
                            AMM(st[2], b0, b0 + 16, b0, b0 + 16, b0, b0 + 16, b0, b0 + 16)
                            AMM(st[3], b0, b0 + 32, b0 + 16, b0 + 32, b0, b0 + 32, b0 + 16, b0 + 32)
                            op("tensor", lambda e, b0=b0, dps=dps, rb=rb, tp=tp, st=st:
                               e.matmul(dps[b0:b0 + 16, b0 + 16:b0 + 32], st[4][rb, b0:b0 + 16],
                                        st[5][rb, b0 + 16:b0 + 32], start=False, stop=True,
                                        tile_position=(tp, b0), skip_group_check=True), inc=(pe, 1))
                        for kb in range(2):
                            b0 = kb * 64
                            op("tensor", lambda e, b0=b0, dps=dps, rb=rb, tp=tp, st=st:
                               e.matmul(dps[b0:b0 + 32, b0 + 32:b0 + 64], st[6][rb, b0:b0 + 32],
                                        st[7][rb, b0 + 32:b0 + 64], start=False, stop=True,
                                        tile_position=(tp, b0), skip_group_check=True), inc=(pe, 1))
                        op("tensor", lambda e, hh=hh, rb=rb, tp=tp, st=st:
                           e.matmul((PS["cA"] if hh == 0 else PS["cB"]), st[12][rb, :], c_ones[rb, 0:1],
                                    start=True, stop=True, tile_position=(tp, 0),
                                    skip_group_check=True), inc=(pe, 1))
                    abuild = W(pe)
                    op("vector", lambda e, st=st: e.select(st[1], c_selmask[:, :], PS["AA"], c_zero[:, :], add_drain=True),
                       waits=abuild)
                    op("vector", lambda e, st=st: e.select(st[2], c_selmask[:, :], PS["AB"], c_zero[:, :], add_drain=True))
                    op("vector", lambda e: e.tensor_copy(escr[:, 0:1], PS["cA"]))
                    op("vector", lambda e: e.tensor_copy(escr[:, 1:2], PS["cB"]))
                    op("vector", lambda e, st=st: e.tensor_copy(st[13][0:64, :], Sst[0:64, :]))
                    aevac = W(ve)
                    for hh in range(2):
                        asb = st[1] if hh == 0 else st[2]
                        rc = st[10] if hh == 0 else st[11]
                        dpo = PS["OA"] if hh == 0 else PS["OB"]
                        op("tensor", lambda e, asb=asb, dpo=dpo, hh=hh, vt=vt:
                           e.matmul(dpo[:, 0:64], asb, vt[:, hh * 64:hh * 64 + 64],
                                    start=True, stop=False, skip_group_check=True),
                           waits=(aevac if hh == 0 else ()), inc=(pe, 1))
                        op("tensor", lambda e, rc=rc, dpo=dpo, hh=hh, st=st:
                           e.matmul(dpo[:, 0:64], rc[0:64, :],
                                    st[13][0:64, hh * 64:hh * 64 + 64],
                                    start=False, stop=True, skip_group_check=True), inc=(pe, 1))
                for hh in range(2):
                    op("tensor", lambda e, hh=hh, vt=vt, st=st:
                       e.matmul((PS["SA"] if hh == 0 else PS["SB"]), st[0][:, hh * 64:hh * 64 + 64],
                                vt[:, hh * 64:hh * 64 + 64], start=True, stop=True,
                                skip_group_check=True),
                       waits=(list(scaled) if boot else []), inc=(pe, 1))
                smm = W(pe)
                if not boot:
                    for hh in range(2):
                        op("vector", lambda e, hh=hh, vt=vt, st=st:
                           e.scalar_tensor_tensor(st[3][:, hh * 64:hh * 64 + 64], vt[:, hh * 64:hh * 64 + 64],
                                                  escr[:, hh:hh + 1], (PS["OA"] if hh == 0 else PS["OB"])[:, 0:64],
                                                  ALU.mult, ALU.add), waits=smm, inc=(ve, 1))
                    op("vector", lambda e, st=st: e.reduce_sum(GNS[:, 0:2], st[3].rearrange("p (s d) -> p s d", s=2), axis=AX),
                       inc=(ve, 1))
                    op("scalar", lambda e, st=st: e.activation(st[4], st[3], AF.Square), waits=W(ve), inc=(ae, 1))
                    op("vector", lambda e, st=st: e.reduce_sum(GNS[:, 2:4], st[4].rearrange("p (s d) -> p s d", s=2), axis=AX),
                       waits=W(ae), inc=(ve, 1))
                    op("vector", lambda e: e.tensor_scalar_mul(GNS[:, 0:2], GNS[:, 0:2], 1.0 / 64), inc=(ve, 1))
                    op("vector", lambda e: e.tensor_scalar_mul(GNS[:, 2:4], GNS[:, 2:4], 1.0 / 64), inc=(ve, 1))
                    op("vector", lambda e: e.tensor_tensor(GNS[:, 4:6], GNS[:, 0:2], GNS[:, 0:2], op=ALU.mult), inc=(ve, 1))
                    op("vector", lambda e: e.tensor_sub(GNS[:, 2:4], GNS[:, 2:4], GNS[:, 4:6]), inc=(ve, 1))
                    op("scalar", lambda e: e.activation(GNS[:, 2:4], GNS[:, 2:4], AF.Sqrt, bias=c_eps[:, 0:1]),
                       waits=W(ve), inc=(ae, 1))
                    op("vector", lambda e: e.reciprocal(GNS[:, 2:4], GNS[:, 2:4]), waits=W(ae), inc=(ve, 1))
                    for hh in range(2):
                        op("vector", lambda e, hh=hh, st=st:
                           e.tensor_scalar(st[3][:, hh * 64:hh * 64 + 64], st[3][:, hh * 64:hh * 64 + 64],
                                           GNS[:, hh:hh + 1], GNS[:, 2 + hh:3 + hh], ALU.subtract, ALU.mult),
                           inc=(ve, 1))
                    op("vector", lambda e, mpr=mpr, st=st: e.tensor_tensor(st[3], st[3], gnw_t[:, mpr * 128:(mpr + 1) * 128],
                                                                    op=ALU.mult), inc=(ve, 1))
                    op("vector", lambda e, mpr=mpr, st=st: e.tensor_add(st[3], st[3], gnb_t[:, mpr * 128:(mpr + 1) * 128]),
                       inc=(ve, 1))
                    op("vector", lambda e, mpr=mpr, n=n, gt=gt, st=st:
                       e.tensor_tensor(gsb[:, (n - 2) * 4096 + mpr * 128:(n - 2) * 4096 + (mpr + 1) * 128],
                                       st[3], gt, op=ALU.mult), waits=W(ae), inc=(ve, 1))
                for hh in range(2):
                    op("vector", lambda e, hh=hh:
                       e.scalar_tensor_tensor(Sst[0:64, hh * 64:hh * 64 + 64], Sst[0:64, hh * 64:hh * 64 + 64],
                                              GNS[0:64, 8 + hh:9 + hh],
                                              (PS["SA"] if hh == 0 else PS["SB"]), ALU.mult, ALU.add),
                       waits=smm, inc=(ve, 1))
        scan_ve = W(ve)
        for cth in range(8):
            op("sync", lambda e, cth=cth: e.dma_start(out=gated_tm[cth * 128:(cth + 1) * 128, :],
                                                      in_=gsb[:, cth * 4096:(cth + 1) * 4096]),
               waits=scan_ve, inc=(gs, 16))
        scan_done = W(gs)

        if debug_stop == "scan":
            for it in range(T // 128):
                op("gpsimd", lambda e, it=it: e.dma_start(out=dbg["gated"][it * 128:(it + 1) * 128, :],
                                                          in_=gated_tm[it * 128:(it + 1) * 128, :]),
                   waits=scan_done, inc=(gd, 16))
            _emit(block, plan, allsems)
            return nc

        gatT = A1.ap()[:, 0:FC * T]
        def gat_sl(j, t0, tl):
            return gatT[:, j * T + t0:j * T + t0 + tl]
        for j in range(FC):
            op("scalar", lambda e, j=j: e.dma_start_transpose(gatT[:, j * T:(j + 1) * T],
                                                              gated_tm[:, j * 128:(j + 1) * 128]),
               waits=scan_done, inc=(tdp[0], 16), dma=True)
        gat_rdy = W(tdp[0])
        oslab = [A2.ap()[:, i * 1024:(i + 1) * 1024] for i in range(2)]
        ost = A2.ap()[:, 2048:2048 + 4096]
        for tt in range(T // 128):
            for iq in range(4):
                i0 = iq * 1024
                for kk in range(FC):
                    bsl = oslab[kk % 2]
                    wsem = wdp[kk % 2]
                    ww = (cast_marks["cdo"] + gat_rdy) if (tt == 0 and iq == 0 and kk < 2) else W(pe)
                    op("sync", lambda e, kk=kk, i0=i0, bsl=bsl:
                       e.dma_start(out=bsl[:, 0:1024], in_=wb["o"][kk * 128:(kk + 1) * 128, i0:i0 + 1024]),
                       waits=ww, inc=(wsem, 16))
                    for nn in range(2):
                        op("tensor", lambda e, tt=tt, kk=kk, nn=nn, bsl=bsl:
                           e.matmul(pA[:, nn * 512:(nn + 1) * 512], gat_sl(kk, tt * 128, 128),
                                    bsl[:, nn * 512:(nn + 1) * 512],
                                    start=(kk == 0), stop=(kk == FC - 1)),
                           waits=W(wsem) + (W(ve) if kk == 0 else []), inc=(pe, 1))
                op("vector", lambda e, iq=iq: e.tensor_copy(ost[:, iq * 1024:(iq + 1) * 1024], pA[:, 0:1024]),
                   waits=W(pe, as6), inc=(ve, 1))
            op("sync", lambda e, tt=tt: e.dma_start(out=attnd_tm[tt * 128:(tt + 1) * 128, :], in_=ost[:, 0:4096]),
               waits=W(ve), inc=(as6, 16))
        attnd_done = W(as6)

        if debug_stop == "attnd":
            for it in range(T // 128):
                op("gpsimd", lambda e, it=it: e.dma_start(out=dbg["attnd"][it * 128:(it + 1) * 128, :],
                                                          in_=attnd_tm[it * 128:(it + 1) * 128, :]),
                   waits=attnd_done, inc=(gd, 16))
            _emit(block, plan, allsems)
            return nc

        hbuf = A1.ap()[:, 0:IC * 512]
        xnTh = A2.ap()[:, 0:FC * 512]
        fslab = [A2.ap()[:, 16384 + i * 2048:16384 + (i + 1) * 2048] for i in range(2)]
        fst = A2.ap()[:, 20480:20480 + 1024]
        xres = A3.ap()[:, 0:2048]
        sres = A3.ap()[:, 2048:4096]
        dpt = [pC[:, 0:512], pC[:, 512:1024], pA[:, 0:512], pA[:, 512:1024]]
        for th in range(2):
            tb = 257 + th * 512
            for j in range(FC):
                op("scalar", lambda e, j=j, tb=tb: e.dma_start_transpose(xnTh[:, j * 512:(j + 1) * 512],
                                                                         xn_tm[tb:tb + 512, j * 128:(j + 1) * 128]),
                   waits=(attnd_done if th == 0 else W(pe, ve)), inc=(tdp[0], 16), dma=True)
            xn_rdy = W(tdp[0])
            for ii in range(IC):
                for mg in ("g", "u"):
                    wsrc = wfb[mg]
                    csem = "cdfg" if mg == "g" else "cdfu"
                    ptt = pA if mg == "g" else pB
                    bsl = fslab[0 if mg == "g" else 1]
                    wsem = wdp[0 if mg == "g" else 1]
                    for kk in range(FC):
                        if kk % 8 == 0:
                            ww = (cast_marks[csem] + xn_rdy) if (ii == 0 and kk == 0 and th == 0) else W(pe)
                            op("sync", lambda e, kk=kk, ii=ii, bsl=bsl, wsrc=wsrc:
                               e.dma_start(out=bsl[:, 0:1024].rearrange("p (a i) -> p a i", a=8),
                                           in_=wsrc[kk * 128:(kk + 8) * 128, ii * 128:(ii + 1) * 128]
                                           .rearrange("(a p) i -> p a i", p=128)),
                               waits=ww, inc=(wsem, 16))
                        op("tensor", lambda e, kk=kk, ptt=ptt, bsl=bsl:
                           e.matmul(ptt[:, 0:512], bsl[:, (kk % 8) * 128:(kk % 8 + 1) * 128],
                                    xnTh[:, kk * 512:(kk + 1) * 512],
                                    start=(kk == 0), stop=(kk == FC - 1)),
                           waits=W(wsem) + (W(ve, ae, tdp[0]) if kk == 0 else []), inc=(pe, 1))
                op("scalar", lambda e: e.activation(fst[:, 0:512], pA[:, 0:512], AF.Silu),
                   waits=W(pe), inc=(ae, 1))
                op("vector", lambda e, ii=ii: e.tensor_tensor(hbuf[:, ii * 512:(ii + 1) * 512],
                                                              fst[:, 0:512], pB[:, 0:512], op=ALU.mult),
                   waits=W(ae), inc=(ve, 1))
            h_rdy = W(ve)
            for ho in range(8):
                hd0 = ho * 512
                for kk in range(IC):
                    bsl = fslab[(kk // 4) % 2]
                    wsem = wdp[(kk // 4) % 2]
                    if kk % 4 == 0:
                        kh = min(4, IC - kk)
                        ww = (cast_marks["cdfd"] + h_rdy) if (ho == 0 and kk == 0) else W(pe)
                        op("sync", lambda e, kk=kk, kh=kh, hd0=hd0, bsl=bsl:
                           e.dma_start(out=bsl[:, 0:kh * 512].rearrange("p (a h) -> p a h", h=512),
                                       in_=wfb["d"][kk * 128:(kk + kh) * 128, hd0:hd0 + 512]
                                       .rearrange("(a p) h -> p a h", p=128)),
                           waits=ww, inc=(wsem, 16))
                    for tt4 in range(4):
                        op("tensor", lambda e, kk=kk, tt4=tt4, bsl=bsl:
                           e.matmul(dpt[tt4], hbuf[:, kk * 512 + tt4 * 128:kk * 512 + (tt4 + 1) * 128],
                                    bsl[:, (kk % 4) * 512:(kk % 4 + 1) * 512],
                                    start=(kk == 0), stop=(kk == IC - 1)),
                           waits=W(wsem) + (W(ve) if kk == 0 else []), inc=(pe, 1))
                mmd = W(pe)
                for tt4 in range(4):
                    r0 = th * 512 + tt4 * 128
                    op("gpsimd", lambda e, r0=r0, hd0=hd0, tt4=tt4:
                       e.dma_start(out=xres[:, tt4 * 512:(tt4 + 1) * 512],
                                   in_=x_in[257 + r0:257 + r0 + 128, hd0:hd0 + 512]),
                       waits=mmd + W(ve), inc=(gd, 16))
                    op("sync", lambda e, r0=r0, hd0=hd0: e.dma_start(out=fst[:, 512:1024],
                                                                     in_=attnd_tm[r0:r0 + 128, hd0:hd0 + 512]),
                       waits=mmd + W(ve), inc=(ssp[0], 16))
                    op("vector", lambda e, tt4=tt4: e.tensor_tensor(sres[:, 0:512], xres[:, tt4 * 512:(tt4 + 1) * 512],
                                                                    fst[:, 512:1024], op=ALU.add),
                       waits=W(gd) + W(ssp[0]) + W(od), inc=(ve, 1))
                    op("vector", lambda e, tt4=tt4: e.tensor_tensor(sres[:, 512:1024], sres[:, 0:512], dpt[tt4],
                                                                    op=ALU.add), inc=(ve, 1))
                    op("sync", lambda e, r0=r0, hd0=hd0: e.dma_start(out=y_out[r0:r0 + 128, hd0:hd0 + 512],
                                                                     in_=sres[:, 512:1024]),
                       waits=W(ve), inc=(od, 16))
        _emit(block, plan, allsems)
    return nc


def _emit(block, plan, allsems):
    def tail(e):
        for s_ in allsems:
            if s_.n:
                e.wait_ge(s_.h, s_.n)
        return None
    plan["sync"].append((tail, [], None))

    def runner(name):
        def go(eng):
            for fn, waits, inc in plan[name]:
                for s_, v_ in waits:
                    if v_ > 0:
                        eng.wait_ge(s_.h, v_)
                ins = fn(eng)
                if inc is not None:
                    s_, k_ = inc
                    assert ins is not None, name
                    ins.then_inc(s_.h, k_)
        return go
    block.sync(runner("sync"))
    block.scalar(runner("scalar"))
    block.vector(runner("vector"))
    block.tensor(runner("tensor"))
    block.gpsimd(runner("gpsimd"))


HOST_SHARED = {}


def prep_shared(full):
    t = lambda w: np.ascontiguousarray(np.asarray(w).T)
    HOST_SHARED["wkT"] = t(full["L0_attn_w_key"])
    HOST_SHARED["wvT"] = t(full["L0_attn_w_value"])
    HOST_SHARED["wrT"] = t(full["L0_attn_w_receptance"])
    HOST_SHARED["wgT"] = t(full["L0_attn_w_gate"])
    HOST_SHARED["woT"] = t(full["L0_attn_w_output"])
    cb = np.asarray(full["L0_ffn_codebook"])
    def deq(idx):
        idx = np.asarray(idx)
        return cb[idx].reshape(idx.shape[0], -1)
    HOST_SHARED["wgT2"] = t(deq(full["L0_ffn_idx_gate"]))
    HOST_SHARED["wuT2"] = t(deq(full["L0_ffn_idx_up"]))
    HOST_SHARED["wdT2"] = t(deq(full["L0_ffn_idx_down"]))


def host_inputs_for_core(c, full):
    b, hf = c // 2, c % 2
    x = np.asarray(full["L0_attn_hidden"][b])
    if hf == 0:
        xe = np.concatenate([x[0:257], x[0:1024], x[0:127]], axis=0)
    else:
        xe = np.concatenate([x[767:2048], x[0:127]], axis=0)
    pp = lambda v: np.ascontiguousarray(np.asarray(v).reshape(FC, 128).T)
    d = {
        "x": np.ascontiguousarray(xe),
        "mask": np.full((128, 1), float(hf), np.float32),
        "mxk": pp(full["L0_attn_time_mix_key"]),
        "mxv": pp(full["L0_attn_time_mix_value"]),
        "mxr": pp(full["L0_attn_time_mix_receptance"]),
        "mxg": pp(full["L0_attn_time_mix_gate"]),
        "tdec": pp(np.asarray(full["L0_attn_time_decay"]).reshape(-1)),
        "tdecr": np.ascontiguousarray(np.asarray(full["L0_attn_time_decay"]).reshape(1, -1)),
        "tfaa": pp(np.asarray(full["L0_attn_time_faaaa"]).reshape(-1)),
        "gnw": np.ascontiguousarray(np.asarray(full["L0_attn_gn_weight"]).reshape(1, -1)),
        "gnb": np.ascontiguousarray(np.asarray(full["L0_attn_gn_bias"]).reshape(1, -1)),
    }
    for kk in ("wkT", "wvT", "wrT", "wgT", "woT", "wgT2", "wuT2", "wdT2"):
        d[kk] = HOST_SHARED[kk]
    return d


def kernel(**inputs):
    from concourse.bass_utils import run_bass_kernel_spmd
    full = {k: np.asarray(v) for k, v in inputs.items()}
    prep_shared(full)
    nc = build_program()
    in_maps = [host_inputs_for_core(c, full) for c in range(8)]
    res = run_bass_kernel_spmd(nc, in_maps, core_ids=list(range(8)))
    out = np.empty((4, 2048, 4096), np.float32)
    for c in range(8):
        b, hf = c // 2, c % 2
        out[b, hf * 1024:(hf + 1) * 1024] = res.results[c]["y"]
    HOST_SHARED.clear()
    return out


# version 03
# speedup vs baseline: 2.0525x; 2.0525x over previous
import numpy as np
import sys
if '/opt/trn_rl_repo' not in sys.path:
    sys.path.insert(0, '/opt/trn_rl_repo')
from contextlib import ExitStack
import concourse.bass as bass
import concourse.mybir as mybir

F32 = mybir.dt.float32
BF16 = mybir.dt.bfloat16
U8 = mybir.dt.uint8
AF = mybir.ActivationFunctionType
ALU = mybir.AluOpType
AX = mybir.AxisListType.X

B, S, HID = 4, 2048, 4096
II = 11008
T = 1024
BT = 256
TE = BT + T
XPAD = 1408
CH = 128
NCH = TE // CH
NPAIR = 32
FC = HID // 128
IC = II // 128
EPS = 1e-5
CLAMP = 41.0

SEMS = ("gd", "xs", "td0", "td1", "ss0", "ss1", "wd0", "wd1", "ps", "gs", "as6",
        "od", "ve", "ae", "pe", "ge", "xl", "wd2", "wd3")


class Sem:
    def __init__(self, h):
        self.h = h
        self.n = 0
    def inc(self, k=1):
        self.n += k
        return self.n


def build_program(debug_stop=None):
    nc = bass.Bass("TRN2", target_bir_lowering=False)

    x_in = nc.dram_tensor("x", [XPAD, HID], F32, kind="ExternalInput").ap()
    mask_in = nc.dram_tensor("mask", [128, 1], F32, kind="ExternalInput").ap()
    wT = {m: nc.dram_tensor(f"w{m}T", [HID, HID], F32, kind="ExternalInput").ap()
          for m in ("k", "v", "r", "g", "o")}
    mx_in = {m: nc.dram_tensor(f"mx{m}", [128, FC], F32, kind="ExternalInput").ap()
             for m in ("k", "v", "r", "g")}
    tdec = nc.dram_tensor("tdec", [128, FC], F32, kind="ExternalInput").ap()
    tdecr = nc.dram_tensor("tdecr", [1, HID], F32, kind="ExternalInput").ap()
    tfaa = nc.dram_tensor("tfaa", [128, FC], F32, kind="ExternalInput").ap()
    gnw_in = nc.dram_tensor("gnw", [1, HID], F32, kind="ExternalInput").ap()
    gnb_in = nc.dram_tensor("gnb", [1, HID], F32, kind="ExternalInput").ap()
    wgT2 = nc.dram_tensor("wgT2", [HID, II], F32, kind="ExternalInput").ap()
    wuT2 = nc.dram_tensor("wuT2", [HID, II], F32, kind="ExternalInput").ap()
    wdT2 = nc.dram_tensor("wdT2", [II, HID], F32, kind="ExternalInput").ap()
    y_out = nc.dram_tensor("y", [T, HID], F32, kind="ExternalOutput").ap()

    xn_tm = nc.dram_tensor("xn_tm", [XPAD, HID], BF16).ap()
    kT_d = nc.dram_tensor("kT_d", [HID, TE], BF16).ap()
    vT_d = nc.dram_tensor("vT_d", [HID, TE], BF16).ap()
    rT_d = nc.dram_tensor("rT_d", [HID, T], BF16).ap()
    gT_d = nc.dram_tensor("gT_d", [HID, T], BF16).ap()
    gated_tm = nc.dram_tensor("gated_tm", [T, HID], BF16).ap()
    attnd_tm = nc.dram_tensor("attnd_tm", [T, HID], BF16).ap()
    h_d = nc.dram_tensor("h_d", [II, 1024], BF16).ap()

    dbg = {}
    if debug_stop == "ln":
        dbg["xn"] = nc.dram_tensor("dbg_xn", [XPAD, HID], F32, kind="ExternalOutput").ap()
    if debug_stop == "proj":
        for nmm, ln_ in (("kT", TE), ("vT", TE), ("rT", T), ("gT", T)):
            dbg[nmm] = nc.dram_tensor("dbg_" + nmm, [HID, ln_], F32, kind="ExternalOutput").ap()
    if debug_stop == "scan":
        dbg["gated"] = nc.dram_tensor("dbg_gated", [T, HID], F32, kind="ExternalOutput").ap()
    if debug_stop == "attnd":
        dbg["attnd"] = nc.dram_tensor("dbg_attnd", [T, HID], F32, kind="ExternalOutput").ap()

    es = ExitStack()
    with es:
        sbt = lambda n_, s_, d_: es.enter_context(nc.sbuf_tensor(n_, s_, d_))
        pst = lambda n_, s_: es.enter_context(nc.psum_tensor(n_, s_, F32))
        c_mask = sbt("c_mask", [128, 1], F32)
        c_mx = sbt("c_mx", [128, 4 * FC], F32)
        c_L = sbt("c_L", [128, FC], F32)
        c_u = sbt("c_u", [128, FC], F32)
        c_ones = sbt("c_ones", [128, 1], BF16)
        c_eps = sbt("c_eps", [128, 1], F32)
        c_onesrow = sbt("c_onesrow", [1, 128], F32)
        c_zero = sbt("c_zero", [128, 128], BF16)
        c_selmask = sbt("c_selmask", [128, 128], U8)
        c_pat = sbt("c_pat", [128, 1536], F32)
        A1 = sbt("A1", [128, 49152], BF16)
        A2 = sbt("A2", [128, 24576], BF16)
        A3 = sbt("A3", [128, 8192], F32)
        pA = pst("pA", [128, 1536])
        pB = pst("pB", [128, 1536])
        pC = pst("pC", [128, 1024])
        _s = {k: es.enter_context(nc.semaphore("s_" + k)) for k in SEMS}
        block = es.enter_context(nc.Block())

        sm = {k: Sem(_s[k]) for k in SEMS}
        gd, xs, ps, gs, as6, od = sm["gd"], sm["xs"], sm["ps"], sm["gs"], sm["as6"], sm["od"]
        ve, ae, pe, ge = sm["ve"], sm["ae"], sm["pe"], sm["ge"]
        tdp = (sm["td0"], sm["td1"])
        ssp = (sm["ss0"], sm["ss1"])
        wdp = (sm["wd0"], sm["wd1"])
        allsems = tuple(sm.values())

        plan = {e: [] for e in ("sync", "scalar", "vector", "tensor", "gpsimd")}

        def op(eng, fn, waits=(), inc=None, dma=False):
            waits = [(w[0], w[1]) for w in waits]
            if not dma:
                if eng == "vector":
                    assert inc is None or inc[0] is ve
                    waits.append((ve, ve.n))
                    inc = (ve, 1)
                elif eng == "scalar":
                    assert inc is None or inc[0] is ae
                    waits.append((ae, ae.n))
                    inc = (ae, 1)
                elif eng == "gpsimd" and inc is not None and inc[0] is ge:
                    waits.append((ge, ge.n))
            if inc is not None:
                inc[0].inc(inc[1])
            plan[eng].append((fn, waits, inc))

        def W(*sems_):
            return [(s_, s_.n) for s_ in sems_]

        op("gpsimd", lambda e: e.dma_start(out=c_mask[:, :], in_=mask_in), inc=(gd, 16))
        for i, m in enumerate(("k", "v", "r", "g")):
            op("gpsimd", lambda e, i=i, m=m: e.dma_start(out=c_mx[:, i * FC:(i + 1) * FC], in_=mx_in[m]), inc=(gd, 16))
        op("gpsimd", lambda e: e.dma_start(out=c_L[:, :], in_=tdec), inc=(gd, 16))
        op("gpsimd", lambda e: e.dma_start(out=c_u[:, :], in_=tfaa), inc=(gd, 16))
        cl = W(gd)
        op("scalar", lambda e: e.activation(c_L[:, :], c_L[:, :], AF.Exp), waits=cl, inc=(ae, 1))
        op("vector", lambda e: e.tensor_scalar_mul(c_L[:, :], c_L[:, :], -1.0), waits=W(ae), inc=(ve, 1))
        op("vector", lambda e: e.memset(c_onesrow[0:1, :], 1.0), inc=(ve, 1))
        op("vector", lambda e: e.memset(c_ones[:, :], 1.0), inc=(ve, 1))
        op("vector", lambda e: e.memset(c_eps[:, :], EPS), inc=(ve, 1))
        op("vector", lambda e: e.memset(c_zero[:, :], 0.0), inc=(ve, 1))
        i_scr = c_pat[:, 1408:1536]
        op("gpsimd", lambda e: e.iota(i_scr, pattern=[[1, 128]], base=0, channel_multiplier=-1,
                                      allow_small_or_imprecise_dtypes=True), inc=(ge, 1))
        op("vector", lambda e: e.tensor_single_scalar(c_selmask[:, :], i_scr, 0.0, ALU.is_gt),
           waits=W(ge), inc=(ve, 1))
        pats = {
            0: ([[0, 8], [1, 16]], -8, 0),
            1: ([[0, 8], [-1, 16]], 7, 0),
            3: ([[0, 4], [-1, 32]], 15, 0),
            4: ([[0, 4], [1, 32]], -16, 0),
            5: ([[0, 2], [-1, 64]], 31, 0),
            6: ([[0, 2], [1, 64]], -32, 0),
            7: ([[3000, 2], [-1, 64]], 63, 0),
            8: ([[-3000, 2], [1, 64]], 3000, 0),
            9: ([[1, 128]], 0, 0),
            10: ([[0, 128]], 127, -1),
        }
        def pat(s_):
            return c_pat[:, s_ * 128:(s_ + 1) * 128]
        for s_, (pp_, bb_, cm_) in pats.items():
            op("gpsimd", lambda e, s_=s_, pp_=pp_, bb_=bb_, cm_=cm_:
               e.iota(pat(s_), pattern=pp_, base=bb_, channel_multiplier=cm_,
                      allow_small_or_imprecise_dtypes=True), inc=(ge, 1))
        op("gpsimd", lambda e: e.iota(pat(2), pattern=[[0, 4], [-3000, 2], [0, 16]], base=3000,
                                      channel_multiplier=0, allow_small_or_imprecise_dtypes=True), inc=(ge, 1))
        op("vector", lambda e: e.tensor_add(pat(2), pat(2), pat(1)), waits=W(ge), inc=(ve, 1))
        ph0 = W(ve, ge, ae, gd)

        xa = A3.ap()[:, 0:4096]
        xsq = A3.ap()[:, 4096:8192]
        st_mu = c_pat[:, 1408:1409]
        st_s2 = c_pat[:, 1409:1410]
        st_rs = c_pat[:, 1410:1411]
        st_t1 = c_pat[:, 1411:1412]
        xnst = A2.ap()[:, 0:4096]
        for it in range(XPAD // 128):
            lw = list(ph0) if it == 0 else W(ve, ae)
            op("sync", lambda e, it=it: e.dma_start(out=xa, in_=x_in[it * 128:(it + 1) * 128, :]),
               waits=lw, inc=(sm["xl"], 16))
            ld = W(sm["xl"])
            op("vector", lambda e: e.reduce_sum(st_mu, xa, axis=AX), waits=ld, inc=(ve, 1))
            op("scalar", lambda e: e.activation(xsq, xa, AF.Square), waits=ld, inc=(ae, 1))
            op("vector", lambda e: e.reduce_sum(st_s2, xsq, axis=AX), waits=W(ae), inc=(ve, 1))
            op("vector", lambda e: e.tensor_scalar_mul(st_mu, st_mu, 1.0 / HID), inc=(ve, 1))
            op("vector", lambda e: e.tensor_scalar_mul(st_s2, st_s2, 1.0 / HID), inc=(ve, 1))
            op("vector", lambda e: e.tensor_tensor(st_t1, st_mu, st_mu, op=ALU.mult), inc=(ve, 1))
            op("vector", lambda e: e.tensor_sub(st_s2, st_s2, st_t1), inc=(ve, 1))
            op("scalar", lambda e: e.activation(st_s2, st_s2, AF.Sqrt, bias=c_eps[:, 0:1]), waits=W(ve), inc=(ae, 1))
            op("vector", lambda e: e.reciprocal(st_rs, st_s2), waits=W(ae), inc=(ve, 1))
            op("vector", lambda e: e.tensor_scalar(xnst, xa, st_mu, st_rs, ALU.subtract, ALU.mult),
               waits=W(xs), inc=(ve, 1))
            op("sync", lambda e, it=it: e.dma_start(out=xn_tm[it * 128:(it + 1) * 128, :], in_=xnst),
               waits=W(ve), inc=(xs, 16))
        op("sync", lambda e: e.dma_start(out=xnst[0:1, :], in_=xn_tm[256:257, :]), waits=W(xs), inc=(sm["xl"], 16))
        op("vector", lambda e: e.tensor_scalar_mul(xnst[0:1, :], xnst[0:1, :], c_mask[0:1, 0:1]),
           waits=W(sm["xl"]), inc=(ve, 1))
        op("sync", lambda e: e.dma_start(out=xn_tm[256:257, :], in_=xnst[0:1, :]), waits=W(ve), inc=(xs, 16))
        xn_ready = W(xs)

        if debug_stop == "ln":
            for it in range(XPAD // 128):
                op("gpsimd", lambda e, it=it: e.dma_start(out=dbg["xn"][it * 128:(it + 1) * 128, :],
                                                          in_=xn_tm[it * 128:(it + 1) * 128, :]),
                   waits=xn_ready, inc=(gd, 16))
            _emit(block, plan, allsems)
            return nc

        mixT = A1.ap()[:, 0:FC * TE]
        def mix_sl(j, t0, tl):
            return mixT[:, j * TE + t0: j * TE + t0 + tl]
        xnc = [A2.ap()[:, 0:1408], A2.ap()[:, 1408:2816]]
        delta = A2.ap()[:, 2816:4224]
        slab = [A2.ap()[:, 4224 + i * 1024:4224 + (i + 1) * 1024] for i in range(2)]
        evst = A2.ap()[:, 6272:6272 + 1280]
        mix_idx = {"k": 0, "v": 1, "r": 2, "g": 3}
        projs = [("k", TE, 0, kT_d), ("v", TE, 0, vT_d), ("r", T, BT, rT_d), ("g", T, BT, gT_d)]
        slab_last_use = [W(pe), W(pe)]
        for pi, (m, TL, toff, dst_d) in enumerate(projs):
            for j in range(FC):
                bsl = xnc[j % 2]
                tsem = tdp[j % 2]
                ww = (xn_ready if (pi == 0 and j < 2) else W(ve))
                op("scalar", lambda e, j=j, bsl=bsl: e.dma_start_transpose(bsl, xn_tm[:, j * 128:(j + 1) * 128]),
                   waits=ww, inc=(tsem, 16), dma=True)
                op("vector", lambda e, bsl=bsl, toff=toff, TL=TL:
                   e.tensor_sub(delta[:, 0:TL], bsl[:, 1 + toff:1 + toff + TL], bsl[:, toff:toff + TL]),
                   waits=W(tsem), inc=(ve, 1))
                op("vector", lambda e, j=j, m=m, bsl=bsl, toff=toff, TL=TL:
                   e.scalar_tensor_tensor(mix_sl(j, 0, TL), delta[:, 0:TL],
                                          c_mx[:, mix_idx[m] * FC + j: mix_idx[m] * FC + j + 1],
                                          bsl[:, toff:toff + TL], ALU.mult, ALU.add), inc=(ve, 1))
            mix_done = W(ve)
            for imac in range(HID // 256):
                i0 = imac * 256
                for kk in range(FC):
                    bpar = (kk // 4) % 2
                    bsl = slab[bpar]
                    wsem = wdp[bpar]
                    if kk % 4 == 0:
                        ww = list(slab_last_use[bpar]) + (list(mix_done) if (imac == 0 and kk < 8) else [])
                        op("gpsimd", lambda e, m=m, kk=kk, i0=i0, bsl=bsl:
                           e.dma_start(out=bsl[:, 0:1024].rearrange("p (a i) -> p a i", a=4),
                                       in_=wT[m][kk * 128:(kk + 4) * 128, i0:i0 + 256]
                                       .rearrange("(a p) i -> p a i", p=128)),
                           waits=ww, inc=(wsem, 16))
                    for q in range(2):
                        pt = (pA, pB)[q]
                        for nn in range((TL + 511) // 512):
                            n0, n1 = nn * 512, min((nn + 1) * 512, TL)
                            op("tensor", lambda e, kk=kk, q=q, pt=pt, n0=n0, n1=n1, bsl=bsl:
                               e.matmul(pt[:, n0:n1],
                                        bsl[:, (kk % 4) * 256 + q * 128:(kk % 4) * 256 + (q + 1) * 128],
                                        mix_sl(kk, n0, n1 - n0),
                                        start=(kk == 0), stop=(kk == FC - 1)),
                               waits=W(wsem) + (W(ve) if kk == 0 else []), inc=(pe, 1))
                    if kk % 4 == 3:
                        slab_last_use[bpar] = W(pe)
                for q in range(2):
                    pt = (pA, pB)[q]
                    op("vector", lambda e, pt=pt, TL=TL: e.tensor_copy(evst[:, 0:TL], pt[:, 0:TL]),
                       waits=W(pe, ps), inc=(ve, 1))
                    op("sync", lambda e, i0=i0, q=q, dst_d=dst_d, TL=TL:
                       e.dma_start(out=dst_d[i0 + q * 128:i0 + (q + 1) * 128, 0:TL], in_=evst[:, 0:TL]),
                       waits=W(ve), inc=(ps, 16))
        proj_done = W(ps)

        if debug_stop == "proj":
            for nmm, src_d, TL in (("kT", kT_d, TE), ("vT", vT_d, TE), ("rT", rT_d, T), ("gT", gT_d, T)):
                for it in range(FC):
                    op("gpsimd", lambda e, nmm=nmm, src_d=src_d, it=it, TL=TL:
                       e.dma_start(out=dbg[nmm][it * 128:(it + 1) * 128, :], in_=src_d[it * 128:(it + 1) * 128, :]),
                       waits=proj_done, inc=(gd, 16))
            _emit(block, plan, allsems)
            return nc

        Lrow = A3.ap()[:, 0:4096]
        xrow = A3.ap()[0:1, 4096:8192]
        Sst = A3.ap()[0:64, 4224:4352]
        escr = A3.ap()[:, 4352:4480]
        GNS = A3.ap()[:, 4480:4608]
        gsb = A1.ap()[:, 0:32768]
        gnw_t = A1.ap()[:, 32768:36864]
        gnb_t = A1.ap()[:, 36864:40960]
        pwp = [A1.ap()[:, 40960 + i * 128: 40960 + (i + 1) * 128] for i in range(12)]
        def a2v(i_):
            return A2.ap()[:, i_ * 128:(i_ + 1) * 128]
        rfb = [a2v(0), a2v(1)]
        kfb = [a2v(2), a2v(3)]
        ktb = [a2v(4), a2v(5)]
        vtb = [a2v(6), a2v(7)]
        gtb = [a2v(8), a2v(9)]
        stt = [[a2v(10 + p * 14 + i) for i in range(14)] for p in range(2)]

        op("gpsimd", lambda e: e.dma_start(out=xrow[0:1, 0:HID], in_=tdecr), waits=proj_done, inc=(gd, 16))
        op("scalar", lambda e: e.activation(xrow[0:1, 0:HID], xrow[0:1, 0:HID], AF.Exp), waits=W(gd), inc=(ae, 1))
        op("vector", lambda e: e.tensor_scalar_mul(xrow[0:1, 0:HID], xrow[0:1, 0:HID], -1.0), waits=W(ae), inc=(ve, 1))
        for q in range(8):
            op("tensor", lambda e, q=q: e.matmul(pC[:, 0:512], c_onesrow[0:1, :], xrow[0:1, q * 512:(q + 1) * 512],
                                                 start=True, stop=True), waits=W(ve), inc=(pe, 1))
            op("vector", lambda e, q=q: e.tensor_copy(Lrow[:, q * 512:(q + 1) * 512], pC[:, 0:512]),
               waits=W(pe), inc=(ve, 1))
        for (dst_t, src_r) in ((gnw_t, gnw_in), (gnb_t, gnb_in)):
            op("gpsimd", lambda e, src_r=src_r: e.dma_start(out=xrow[0:1, 0:HID], in_=src_r), waits=W(ve), inc=(gd, 16))
            for q in range(8):
                op("tensor", lambda e, q=q: e.matmul(pC[:, 0:512], c_onesrow[0:1, :], xrow[0:1, q * 512:(q + 1) * 512],
                                                     start=True, stop=True), waits=W(gd, ve), inc=(pe, 1))
                op("vector", lambda e, dst_t=dst_t, q=q: e.tensor_copy(dst_t[:, q * 512:(q + 1) * 512], pC[:, 0:512]),
                   waits=W(pe), inc=(ve, 1))
        scan_pre = W(ve, pe, ae, gd)

        PS = {"AA": pA[:, 0:128],
              "AB": pA[:, 512:640],
              "OA": pA[:, 1024:1088],
              "OB": pB[:, 0:64],
              "cA": pB[:, 512:513],
              "cB": pB[:, 1024:1025],
              "SA": pC[0:64, 0:64],
              "SB": pC[0:64, 512:576]}

        for mpr in range(NPAIR):
            Lcol = c_L[:, mpr:mpr + 1]
            ucol = c_u[:, mpr:mpr + 1]
            first_w = scan_pre if mpr == 0 else []
            def mkpow(dst_t, pslot, extra=()):
                op("vector", lambda e, pslot=pslot, Lcol=Lcol:
                   e.tensor_scalar(escr, pat(pslot), Lcol, CLAMP, ALU.mult, ALU.min),
                   waits=list(extra) + W(ae, ve), inc=(ve, 1))
                op("scalar", lambda e, dst_t=dst_t: e.activation(dst_t, escr, AF.Exp),
                   waits=W(ve), inc=(ae, 1))
            for slot, pslot in ((0, 0), (1, 1), (2, 2), (3, 3), (4, 4), (5, 5), (6, 6), (7, 7), (8, 8), (9, 9)):
                mkpow(pwp[slot], pslot, extra=first_w if slot == 0 else ())
            op("vector", lambda e, mpr=mpr:
               e.tensor_tensor(escr, pat(10), Lrow[:, mpr * 128:(mpr + 1) * 128], op=ALU.mult),
               waits=W(ae, ve), inc=(ve, 1))
            op("scalar", lambda e: e.activation(pwp[10], escr, AF.Exp), waits=W(ve), inc=(ae, 1))
            op("vector", lambda e, Lcol=Lcol: e.tensor_single_scalar(GNS[:, 8:9], Lcol, 128.0, ALU.mult),
               waits=W(ae), inc=(ve, 1))
            op("scalar", lambda e: e.activation(GNS[:, 8:9], GNS[:, 8:9], AF.Exp), waits=W(ve), inc=(ae, 1))
            op("vector", lambda e: e.tensor_copy(GNS[0:64, 9:10], GNS[64:128, 8:9]), waits=W(ae), inc=(ve, 1))
            pows = W(ae, ve)
            op("vector", lambda e: e.memset(Sst[:, :], 0.0), waits=W(pe), inc=(ve, 1))

            for n in range(NCH):
                cpar = n % 2
                boot = n < 2
                st = stt[cpar]
                rf, kf, kt, vt, gt = rfb[cpar], kfb[cpar], ktb[cpar], vtb[cpar], gtb[cpar]
                tsem, ssem = tdp[cpar], ssp[cpar]
                t0e = n * CH
                ldw = (pows if n < 2 else W(pe, ve))
                op("scalar", lambda e, mpr=mpr, t0e=t0e, kt=kt:
                   e.dma_start_transpose(kt, kT_d[mpr * 128:(mpr + 1) * 128, t0e:t0e + CH]), waits=ldw, inc=(tsem, 16), dma=True)
                op("scalar", lambda e, mpr=mpr, t0e=t0e, vt=vt:
                   e.dma_start_transpose(vt, vT_d[mpr * 128:(mpr + 1) * 128, t0e:t0e + CH]), inc=(tsem, 16), dma=True)
                op("sync", lambda e, mpr=mpr, t0e=t0e, kf=kf:
                   e.dma_start(out=kf, in_=kT_d[mpr * 128:(mpr + 1) * 128, t0e:t0e + CH]), waits=ldw, inc=(ssem, 16))
                if not boot:
                    top = t0e - BT
                    op("scalar", lambda e, mpr=mpr, top=top, gt=gt:
                       e.dma_start_transpose(gt, gT_d[mpr * 128:(mpr + 1) * 128, top:top + CH]), inc=(tsem, 16), dma=True)
                    op("sync", lambda e, mpr=mpr, top=top, rf=rf:
                       e.dma_start(out=rf, in_=rT_d[mpr * 128:(mpr + 1) * 128, top:top + CH]), inc=(ssem, 16))
                ch_ld = W(tsem, ssem)
                if boot:
                    op("vector", lambda e, kt=kt: e.tensor_scalar_mul(kt, kt, c_mask[:, 0:1]),
                       waits=ch_ld, inc=(ve, 1))
                    ch_ld = W(tsem, ssem, ve)
                else:
                    op("scalar", lambda e, gt=gt: e.activation(gt, gt, AF.Silu), waits=ch_ld, inc=(ae, 1))
                op("vector", lambda e, kt=kt, st=st: e.tensor_tensor(st[0], kt, pwp[10], op=ALU.mult),
                   waits=ch_ld, inc=(ve, 1))
                if not boot:
                    op("vector", lambda e, rf=rf, st=st: e.tensor_tensor(st[1], rf, pwp[0], op=ALU.mult), inc=(ve, 1))
                    op("vector", lambda e, kf=kf, st=st: e.tensor_tensor(st[2], kf, pwp[1], op=ALU.mult), inc=(ve, 1))
                    op("vector", lambda e, kf=kf, st=st: e.tensor_tensor(st[3], kf, pwp[2], op=ALU.mult), inc=(ve, 1))
                    op("vector", lambda e, kf=kf, st=st: e.tensor_tensor(st[4], kf, pwp[3], op=ALU.mult), inc=(ve, 1))
                    op("vector", lambda e, rf=rf, st=st: e.tensor_tensor(st[5], rf, pwp[4], op=ALU.mult), inc=(ve, 1))
                    op("vector", lambda e, kf=kf, st=st: e.tensor_tensor(st[6], kf, pwp[5], op=ALU.mult), inc=(ve, 1))
                    op("vector", lambda e, rf=rf, st=st: e.tensor_tensor(st[7], rf, pwp[6], op=ALU.mult), inc=(ve, 1))
                    op("vector", lambda e, kf=kf, st=st: e.tensor_tensor(st[8], kf, pwp[7], op=ALU.mult), inc=(ve, 1))
                    op("vector", lambda e, rf=rf, st=st: e.tensor_tensor(st[9], rf, pwp[8], op=ALU.mult), inc=(ve, 1))
                    op("vector", lambda e, rf=rf, st=st: e.tensor_tensor(st[10][0:64, :], rf[0:64, :], pwp[9][0:64, :], op=ALU.mult), inc=(ve, 1))
                    op("vector", lambda e, rf=rf, st=st: e.tensor_tensor(st[11][0:64, :], rf[64:128, :], pwp[9][64:128, :], op=ALU.mult), inc=(ve, 1))
                    op("vector", lambda e, rf=rf, st=st, ucol=ucol: e.tensor_scalar_mul(st[12], rf, ucol), inc=(ve, 1))
                    op("vector", lambda e, kf=kf, st=st: e.tensor_tensor(st[12], st[12], kf, op=ALU.mult), inc=(ve, 1))
                scaled = W(ve)

                if not boot:
                    for hh in range(2):
                        dps = PS["AA"] if hh == 0 else PS["AB"]
                        rb = slice(hh * 64, hh * 64 + 64)
                        tp = hh * 64
                        op("tensor", lambda e, dps=dps, rb=rb, tp=tp, st=st:
                           e.matmul(dps[0:128, 0:128], st[8][rb, :], st[9][rb, :],
                                    start=True, stop=True, tile_position=(tp, 0),
                                    skip_group_check=True),
                           waits=(scaled if hh == 0 else ()), inc=(pe, 1))
                        def AMM(lhs, l0, l1, r0_, r1_, o0, o1, t0_, t1_, waits_=()):
                            op("tensor", lambda e, lhs=lhs, l0=l0, l1=l1, r0_=r0_, r1_=r1_, o0=o0, o1=o1,
                               t0_=t0_, t1_=t1_, dps=dps, rb=rb, tp=tp, st=st:
                               e.matmul(dps[o0:o1, t0_:t1_], lhs[rb, l0:l1], st[1][rb, r0_:r1_],
                                        start=False, stop=True, tile_position=(tp, o0 - o0 % 32),
                                        skip_group_check=True),
                               waits=list(waits_), inc=(pe, 1))
                        for kb in range(4):
                            b0 = kb * 32
                            AMM(st[2], b0, b0 + 16, b0, b0 + 16, b0, b0 + 16, b0, b0 + 16)
                            AMM(st[3], b0, b0 + 32, b0 + 16, b0 + 32, b0, b0 + 32, b0 + 16, b0 + 32)
                            op("tensor", lambda e, b0=b0, dps=dps, rb=rb, tp=tp, st=st:
                               e.matmul(dps[b0:b0 + 16, b0 + 16:b0 + 32], st[4][rb, b0:b0 + 16],
                                        st[5][rb, b0 + 16:b0 + 32], start=False, stop=True,
                                        tile_position=(tp, b0), skip_group_check=True), inc=(pe, 1))
                        for kb in range(2):
                            b0 = kb * 64
                            op("tensor", lambda e, b0=b0, dps=dps, rb=rb, tp=tp, st=st:
                               e.matmul(dps[b0:b0 + 32, b0 + 32:b0 + 64], st[6][rb, b0:b0 + 32],
                                        st[7][rb, b0 + 32:b0 + 64], start=False, stop=True,
                                        tile_position=(tp, b0), skip_group_check=True), inc=(pe, 1))
                        op("tensor", lambda e, hh=hh, rb=rb, tp=tp, st=st:
                           e.matmul((PS["cA"] if hh == 0 else PS["cB"]), st[12][rb, :], c_ones[rb, 0:1],
                                    start=True, stop=True, tile_position=(tp, 0),
                                    skip_group_check=True), inc=(pe, 1))
                    abuild = W(pe)
                    op("vector", lambda e, st=st: e.select(st[1], c_selmask[:, :], PS["AA"], c_zero[:, :], add_drain=True),
                       waits=abuild)
                    op("vector", lambda e, st=st: e.select(st[2], c_selmask[:, :], PS["AB"], c_zero[:, :], add_drain=True))
                    op("vector", lambda e: e.tensor_copy(escr[:, 0:1], PS["cA"]))
                    op("vector", lambda e: e.tensor_copy(escr[:, 1:2], PS["cB"]))
                    op("vector", lambda e, st=st: e.tensor_copy(st[13][0:64, :], Sst[0:64, :]))
                    aevac = W(ve)
                    for hh in range(2):
                        asb = st[1] if hh == 0 else st[2]
                        rc = st[10] if hh == 0 else st[11]
                        dpo = PS["OA"] if hh == 0 else PS["OB"]
                        op("tensor", lambda e, asb=asb, dpo=dpo, hh=hh, vt=vt:
                           e.matmul(dpo[:, 0:64], asb, vt[:, hh * 64:hh * 64 + 64],
                                    start=True, stop=False, skip_group_check=True),
                           waits=(aevac if hh == 0 else ()), inc=(pe, 1))
                        op("tensor", lambda e, rc=rc, dpo=dpo, hh=hh, st=st:
                           e.matmul(dpo[:, 0:64], rc[0:64, :],
                                    st[13][0:64, hh * 64:hh * 64 + 64],
                                    start=False, stop=True, skip_group_check=True), inc=(pe, 1))
                for hh in range(2):
                    op("tensor", lambda e, hh=hh, vt=vt, st=st:
                       e.matmul((PS["SA"] if hh == 0 else PS["SB"]), st[0][:, hh * 64:hh * 64 + 64],
                                vt[:, hh * 64:hh * 64 + 64], start=True, stop=True,
                                skip_group_check=True),
                       waits=(list(scaled) if boot else []), inc=(pe, 1))
                smm = W(pe)
                if not boot:
                    for hh in range(2):
                        op("vector", lambda e, hh=hh, vt=vt, st=st:
                           e.scalar_tensor_tensor(st[3][:, hh * 64:hh * 64 + 64], vt[:, hh * 64:hh * 64 + 64],
                                                  escr[:, hh:hh + 1], (PS["OA"] if hh == 0 else PS["OB"])[:, 0:64],
                                                  ALU.mult, ALU.add), waits=smm, inc=(ve, 1))
                    op("vector", lambda e, st=st: e.reduce_sum(GNS[:, 0:2], st[3].rearrange("p (s d) -> p s d", s=2), axis=AX),
                       inc=(ve, 1))
                    op("scalar", lambda e, st=st: e.activation(st[4], st[3], AF.Square), waits=W(ve), inc=(ae, 1))
                    op("vector", lambda e, st=st: e.reduce_sum(GNS[:, 2:4], st[4].rearrange("p (s d) -> p s d", s=2), axis=AX),
                       waits=W(ae), inc=(ve, 1))
                    op("vector", lambda e: e.tensor_scalar_mul(GNS[:, 0:2], GNS[:, 0:2], 1.0 / 64), inc=(ve, 1))
                    op("vector", lambda e: e.tensor_scalar_mul(GNS[:, 2:4], GNS[:, 2:4], 1.0 / 64), inc=(ve, 1))
                    op("vector", lambda e: e.tensor_tensor(GNS[:, 4:6], GNS[:, 0:2], GNS[:, 0:2], op=ALU.mult), inc=(ve, 1))
                    op("vector", lambda e: e.tensor_sub(GNS[:, 2:4], GNS[:, 2:4], GNS[:, 4:6]), inc=(ve, 1))
                    op("scalar", lambda e: e.activation(GNS[:, 2:4], GNS[:, 2:4], AF.Sqrt, bias=c_eps[:, 0:1]),
                       waits=W(ve), inc=(ae, 1))
                    op("vector", lambda e: e.reciprocal(GNS[:, 2:4], GNS[:, 2:4]), waits=W(ae), inc=(ve, 1))
                    for hh in range(2):
                        op("vector", lambda e, hh=hh, st=st:
                           e.tensor_scalar(st[3][:, hh * 64:hh * 64 + 64], st[3][:, hh * 64:hh * 64 + 64],
                                           GNS[:, hh:hh + 1], GNS[:, 2 + hh:3 + hh], ALU.subtract, ALU.mult),
                           inc=(ve, 1))
                    op("vector", lambda e, mpr=mpr, st=st: e.tensor_tensor(st[3], st[3], gnw_t[:, mpr * 128:(mpr + 1) * 128],
                                                                    op=ALU.mult), inc=(ve, 1))
                    op("vector", lambda e, mpr=mpr, st=st: e.tensor_add(st[3], st[3], gnb_t[:, mpr * 128:(mpr + 1) * 128]),
                       inc=(ve, 1))
                    op("vector", lambda e, mpr=mpr, n=n, gt=gt, st=st:
                       e.tensor_tensor(gsb[:, (n - 2) * 4096 + mpr * 128:(n - 2) * 4096 + (mpr + 1) * 128],
                                       st[3], gt, op=ALU.mult), waits=W(ae), inc=(ve, 1))
                for hh in range(2):
                    op("vector", lambda e, hh=hh:
                       e.scalar_tensor_tensor(Sst[0:64, hh * 64:hh * 64 + 64], Sst[0:64, hh * 64:hh * 64 + 64],
                                              GNS[0:64, 8 + hh:9 + hh],
                                              (PS["SA"] if hh == 0 else PS["SB"]), ALU.mult, ALU.add),
                       waits=smm, inc=(ve, 1))
        scan_ve = W(ve)
        for cth in range(8):
            op("sync", lambda e, cth=cth: e.dma_start(out=gated_tm[cth * 128:(cth + 1) * 128, :],
                                                      in_=gsb[:, cth * 4096:(cth + 1) * 4096]),
               waits=scan_ve, inc=(gs, 16))
        scan_done = W(gs)

        if debug_stop == "scan":
            for it in range(T // 128):
                op("gpsimd", lambda e, it=it: e.dma_start(out=dbg["gated"][it * 128:(it + 1) * 128, :],
                                                          in_=gated_tm[it * 128:(it + 1) * 128, :]),
                   waits=scan_done, inc=(gd, 16))
            _emit(block, plan, allsems)
            return nc

        gatT = A1.ap()[:, 0:FC * T]
        def gat_sl(j, t0, tl):
            return gatT[:, j * T + t0:j * T + t0 + tl]
        for j in range(FC):
            op("scalar", lambda e, j=j: e.dma_start_transpose(gatT[:, j * T:(j + 1) * T],
                                                              gated_tm[:, j * 128:(j + 1) * 128]),
               waits=scan_done, inc=(tdp[0], 16), dma=True)
        gat_rdy = W(tdp[0])
        oslab = [A2.ap()[:, i * 512:(i + 1) * 512] for i in range(2)]
        ost = A2.ap()[:, 2048:2048 + 4096]
        dpt8 = [pA[:, 0:512], pA[:, 512:1024], pA[:, 1024:1536],
                pB[:, 0:512], pB[:, 512:1024], pB[:, 1024:1536],
                pC[:, 0:512], pC[:, 512:1024]]
        oslab_last = [gat_rdy, gat_rdy]
        for iqh in range(8):
            i0 = iqh * 512
            for kk in range(FC):
                bpar = kk % 2
                bsl = oslab[bpar]
                wsem = wdp[bpar]
                ww = list(oslab_last[bpar]) + (list(gat_rdy) if (iqh == 0 and kk < 2) else [])
                op("gpsimd", lambda e, kk=kk, i0=i0, bsl=bsl:
                   e.dma_start(out=bsl[:, 0:512], in_=wT["o"][kk * 128:(kk + 1) * 128, i0:i0 + 512]),
                   waits=ww, inc=(wsem, 16))
                for tt in range(8):
                    op("tensor", lambda e, tt=tt, kk=kk, bsl=bsl:
                       e.matmul(dpt8[tt], gat_sl(kk, tt * 128, 128), bsl[:, 0:512],
                                start=(kk == 0), stop=(kk == FC - 1)),
                       waits=W(wsem) + (W(ve) if kk == 0 else []), inc=(pe, 1))
                oslab_last[bpar] = W(pe)
            for tt in range(8):
                op("vector", lambda e, tt=tt: e.tensor_copy(ost[:, tt * 512:(tt + 1) * 512], dpt8[tt]),
                   waits=W(pe, as6), inc=(ve, 1))
            for tt in range(8):
                op("sync", lambda e, tt=tt, i0=i0: e.dma_start(out=attnd_tm[tt * 128:(tt + 1) * 128, i0:i0 + 512],
                                                               in_=ost[:, tt * 512:(tt + 1) * 512]),
                   waits=W(ve), inc=(as6, 16))
        attnd_done = W(as6)

        if debug_stop == "attnd":
            for it in range(T // 128):
                op("gpsimd", lambda e, it=it: e.dma_start(out=dbg["attnd"][it * 128:(it + 1) * 128, :],
                                                          in_=attnd_tm[it * 128:(it + 1) * 128, :]),
                   waits=attnd_done, inc=(gd, 16))
            _emit(block, plan, allsems)
            return nc

        xnT6 = A1.ap()[:, 0:FC * 1024]
        fslab = [A2.ap()[:, i * 1024:(i + 1) * 1024] for i in range(4)]
        dslab = [A2.ap()[:, 4096 + i * 2048:4096 + (i + 1) * 2048] for i in range(2)]
        hlt = [A2.ap()[:, 8192 + i * 1024:8192 + (i + 1) * 1024] for i in range(2)]
        fst = A2.ap()[:, 10240:10240 + 1024]
        hst = A2.ap()[:, 11264:11264 + 1024]
        xres = A3.ap()[:, 0:2048]
        sres = A3.ap()[:, 2048:4096]
        for j in range(FC):
            op("scalar", lambda e, j=j: e.dma_start_transpose(xnT6[:, j * 1024:(j + 1) * 1024],
                                                              xn_tm[257:1281, j * 128:(j + 1) * 128]),
               waits=(attnd_done if j < 2 else W(pe, ve)), inc=(tdp[0], 16), dma=True)
        xn_rdy = W(tdp[0])
        fslab_last = [xn_rdy, xn_rdy, xn_rdy, xn_rdy]
        fsems = (wdp[0], wdp[1], sm["wd2"], sm["wd3"])
        for ii in range(IC):
            for mg in ("g", "u"):
                wsrc = wgT2 if mg == "g" else wuT2
                ptt = pA if mg == "g" else pB
                for kk in range(FC):
                    kg = kk // 8
                    bi = (0 if mg == "g" else 2) + kg % 2
                    bsl = fslab[bi]
                    wsem = fsems[bi]
                    if kk % 8 == 0:
                        ww = list(fslab_last[bi]) + (list(xn_rdy) if ii == 0 else [])
                        op("gpsimd", lambda e, kk=kk, ii=ii, bsl=bsl, wsrc=wsrc:
                           e.dma_start(out=bsl[:, 0:1024].rearrange("p (a i) -> p a i", a=8),
                                       in_=wsrc[kk * 128:(kk + 8) * 128, ii * 128:(ii + 1) * 128]
                                       .rearrange("(a p) i -> p a i", p=128)),
                           waits=ww, inc=(wsem, 16))
                    for nn in range(2):
                        op("tensor", lambda e, kk=kk, nn=nn, ptt=ptt, bsl=bsl:
                           e.matmul(ptt[:, nn * 512:(nn + 1) * 512], bsl[:, (kk % 8) * 128:(kk % 8 + 1) * 128],
                                    xnT6[:, kk * 1024 + nn * 512:kk * 1024 + (nn + 1) * 512],
                                    start=(kk == 0), stop=(kk == FC - 1)),
                           waits=W(wsem) + (W(ve, ae) if kk == 0 else []), inc=(pe, 1))
                    if kk % 8 == 7:
                        fslab_last[bi] = W(pe)
            op("scalar", lambda e: e.activation(fst[:, 0:1024], pA[:, 0:1024], AF.Silu),
               waits=W(pe), inc=(ae, 1))
            op("vector", lambda e: e.tensor_tensor(hst[:, 0:1024], fst[:, 0:1024], pB[:, 0:1024],
                                                   op=ALU.mult), waits=W(ae, ps), inc=(ve, 1))
            op("sync", lambda e, ii=ii: e.dma_start(out=h_d[ii * 128:(ii + 1) * 128, :], in_=hst[:, 0:1024]),
               waits=W(ve), inc=(ps, 16))
        h_rdy = W(ve, ps)
        dslab_last = [h_rdy, h_rdy]
        hl_last = [h_rdy, h_rdy]
        for ho in range(8):
            hd0 = ho * 512
            for kk in range(IC):
                bpar = (kk // 4) % 2
                bsl = dslab[bpar]
                wsem = wdp[bpar]
                hpar = kk % 2
                ht = hlt[hpar]
                hsem = ssp[hpar]
                if kk % 4 == 0:
                    kh = min(4, IC - kk)
                    ww = list(dslab_last[bpar]) + (list(h_rdy) if (ho == 0 and kk < 8) else [])
                    op("gpsimd", lambda e, kk=kk, kh=kh, hd0=hd0, bsl=bsl:
                       e.dma_start(out=bsl[:, 0:kh * 512].rearrange("p (a h) -> p a h", h=512),
                                   in_=wdT2[kk * 128:(kk + kh) * 128, hd0:hd0 + 512]
                                   .rearrange("(a p) h -> p a h", p=128)),
                       waits=ww, inc=(wsem, 16))
                op("sync", lambda e, kk=kk, ht=ht: e.dma_start(out=ht, in_=h_d[kk * 128:(kk + 1) * 128, :]),
                   waits=list(hl_last[hpar]), inc=(hsem, 16))
                for tt in range(8):
                    op("tensor", lambda e, kk=kk, tt=tt, bsl=bsl, ht=ht:
                       e.matmul(dpt8[tt], ht[:, tt * 128:(tt + 1) * 128],
                                bsl[:, (kk % 4) * 512:(kk % 4 + 1) * 512],
                                start=(kk == 0), stop=(kk == IC - 1)),
                       waits=W(wsem, hsem) + (W(ve) if kk == 0 else []), inc=(pe, 1))
                if kk % 4 == 3 or kk == IC - 1:
                    dslab_last[bpar] = W(pe)
                hl_last[hpar] = W(pe)
            mmd = W(pe)
            for tt in range(8):
                r0 = tt * 128
                op("sync", lambda e, r0=r0, hd0=hd0, tt=tt:
                   e.dma_start(out=xres[:, (tt % 4) * 512:(tt % 4 + 1) * 512],
                               in_=x_in[257 + r0:257 + r0 + 128, hd0:hd0 + 512]),
                   waits=mmd + W(ve), inc=(sm["xl"], 16))
                op("scalar", lambda e, r0=r0, hd0=hd0: e.dma_start(out=fst[:, 0:512],
                                                                   in_=attnd_tm[r0:r0 + 128, hd0:hd0 + 512]),
                   waits=mmd + W(ve), inc=(tdp[1], 16), dma=True)
                op("vector", lambda e, tt=tt: e.tensor_tensor(sres[:, 0:512], xres[:, (tt % 4) * 512:(tt % 4 + 1) * 512],
                                                              fst[:, 0:512], op=ALU.add),
                   waits=W(sm["xl"]) + W(tdp[1]) + W(od), inc=(ve, 1))
                op("vector", lambda e, tt=tt: e.tensor_tensor(sres[:, 512:1024], sres[:, 0:512], dpt8[tt],
                                                              op=ALU.add), inc=(ve, 1))
                op("sync", lambda e, r0=r0, hd0=hd0: e.dma_start(out=y_out[r0:r0 + 128, hd0:hd0 + 512],
                                                                 in_=sres[:, 512:1024]),
                   waits=W(ve), inc=(od, 16))
        _emit(block, plan, allsems)
    return nc


def _emit(block, plan, allsems):
    def tail(e):
        for s_ in allsems:
            if s_.n:
                e.wait_ge(s_.h, s_.n)
        return None
    plan["sync"].append((tail, [], None))

    def runner(name):
        def go(eng):
            for fn, waits, inc in plan[name]:
                for s_, v_ in waits:
                    if v_ > 0:
                        eng.wait_ge(s_.h, v_)
                ins = fn(eng)
                if inc is not None:
                    s_, k_ = inc
                    assert ins is not None, name
                    ins.then_inc(s_.h, k_)
        return go
    block.sync(runner("sync"))
    block.scalar(runner("scalar"))
    block.vector(runner("vector"))
    block.tensor(runner("tensor"))
    block.gpsimd(runner("gpsimd"))


HOST_SHARED = {}


def prep_shared(full):
    t = lambda w: np.ascontiguousarray(np.asarray(w).T)
    HOST_SHARED["wkT"] = t(full["L0_attn_w_key"])
    HOST_SHARED["wvT"] = t(full["L0_attn_w_value"])
    HOST_SHARED["wrT"] = t(full["L0_attn_w_receptance"])
    HOST_SHARED["wgT"] = t(full["L0_attn_w_gate"])
    HOST_SHARED["woT"] = t(full["L0_attn_w_output"])
    cb = np.asarray(full["L0_ffn_codebook"])
    def deq(idx):
        idx = np.asarray(idx)
        return cb[idx].reshape(idx.shape[0], -1)
    HOST_SHARED["wgT2"] = t(deq(full["L0_ffn_idx_gate"]))
    HOST_SHARED["wuT2"] = t(deq(full["L0_ffn_idx_up"]))
    HOST_SHARED["wdT2"] = t(deq(full["L0_ffn_idx_down"]))


def host_inputs_for_core(c, full):
    b, hf = c // 2, c % 2
    x = np.asarray(full["L0_attn_hidden"][b])
    if hf == 0:
        xe = np.concatenate([x[0:257], x[0:1024], x[0:127]], axis=0)
    else:
        xe = np.concatenate([x[767:2048], x[0:127]], axis=0)
    pp = lambda v: np.ascontiguousarray(np.asarray(v).reshape(FC, 128).T)
    d = {
        "x": np.ascontiguousarray(xe),
        "mask": np.full((128, 1), float(hf), np.float32),
        "mxk": pp(full["L0_attn_time_mix_key"]),
        "mxv": pp(full["L0_attn_time_mix_value"]),
        "mxr": pp(full["L0_attn_time_mix_receptance"]),
        "mxg": pp(full["L0_attn_time_mix_gate"]),
        "tdec": pp(np.asarray(full["L0_attn_time_decay"]).reshape(-1)),
        "tdecr": np.ascontiguousarray(np.asarray(full["L0_attn_time_decay"]).reshape(1, -1)),
        "tfaa": pp(np.asarray(full["L0_attn_time_faaaa"]).reshape(-1)),
        "gnw": np.ascontiguousarray(np.asarray(full["L0_attn_gn_weight"]).reshape(1, -1)),
        "gnb": np.ascontiguousarray(np.asarray(full["L0_attn_gn_bias"]).reshape(1, -1)),
    }
    for kk in ("wkT", "wvT", "wrT", "wgT", "woT", "wgT2", "wuT2", "wdT2"):
        d[kk] = HOST_SHARED[kk]
    return d


def kernel(**inputs):
    from concourse.bass_utils import run_bass_kernel_spmd
    full = {k: np.asarray(v) for k, v in inputs.items()}
    prep_shared(full)
    nc = build_program()
    in_maps = [host_inputs_for_core(c, full) for c in range(8)]
    res = run_bass_kernel_spmd(nc, in_maps, core_ids=list(range(8)))
    out = np.empty((4, 2048, 4096), np.float32)
    for c in range(8):
        b, hf = c // 2, c % 2
        out[b, hf * 1024:(hf + 1) * 1024] = res.results[c]["y"]
    HOST_SHARED.clear()
    return out


# version 04
# speedup vs baseline: 2.2541x; 1.0982x over previous
import numpy as np
import sys
if '/opt/trn_rl_repo' not in sys.path:
    sys.path.insert(0, '/opt/trn_rl_repo')
from contextlib import ExitStack
import concourse.bass as bass
import concourse.mybir as mybir

F32 = mybir.dt.float32
BF16 = mybir.dt.bfloat16
U8 = mybir.dt.uint8
AF = mybir.ActivationFunctionType
ALU = mybir.AluOpType
AX = mybir.AxisListType.X

B, S, HID = 4, 2048, 4096
II = 11008
T = 1024
BT = 256
TE = BT + T
XPAD = 1408
CH = 128
NCH = TE // CH
NPAIR = 32
FC = HID // 128
IC = II // 128
EPS = 1e-5
CLAMP = 41.0

SEMS = ("gd", "xs", "td0", "td1", "ss0", "ss1", "wd0", "wd1", "ps", "gs", "as6",
        "od", "ve", "ae", "pe", "ge", "xl", "wd2", "wd3")


class Sem:
    def __init__(self, h):
        self.h = h
        self.n = 0
    def inc(self, k=1):
        self.n += k
        return self.n


def build_program(debug_stop=None):
    nc = bass.Bass("TRN2", target_bir_lowering=False)

    x_in = nc.dram_tensor("x", [XPAD, HID], F32, kind="ExternalInput").ap()
    mask_in = nc.dram_tensor("mask", [128, 1], F32, kind="ExternalInput").ap()
    wT = {m: nc.dram_tensor(f"w{m}T", [HID, HID], F32, kind="ExternalInput").ap()
          for m in ("k", "v", "r", "g", "o")}
    mx_in = {m: nc.dram_tensor(f"mx{m}", [128, FC], F32, kind="ExternalInput").ap()
             for m in ("k", "v", "r", "g")}
    tdec = nc.dram_tensor("tdec", [128, FC], F32, kind="ExternalInput").ap()
    tdecr = nc.dram_tensor("tdecr", [1, HID], F32, kind="ExternalInput").ap()
    tfaa = nc.dram_tensor("tfaa", [128, FC], F32, kind="ExternalInput").ap()
    gnw_in = nc.dram_tensor("gnw", [1, HID], F32, kind="ExternalInput").ap()
    gnb_in = nc.dram_tensor("gnb", [1, HID], F32, kind="ExternalInput").ap()
    wgT2 = nc.dram_tensor("wgT2", [HID, II], F32, kind="ExternalInput").ap()
    wuT2 = nc.dram_tensor("wuT2", [HID, II], F32, kind="ExternalInput").ap()
    wdT2 = nc.dram_tensor("wdT2", [II, HID], F32, kind="ExternalInput").ap()
    y_out = nc.dram_tensor("y", [T, HID], F32, kind="ExternalOutput").ap()

    xn_tm = nc.dram_tensor("xn_tm", [XPAD, HID], BF16).ap()
    kT_d = nc.dram_tensor("kT_d", [HID, TE], BF16).ap()
    vT_d = nc.dram_tensor("vT_d", [HID, TE], BF16).ap()
    rT_d = nc.dram_tensor("rT_d", [HID, T], BF16).ap()
    gT_d = nc.dram_tensor("gT_d", [HID, T], BF16).ap()
    gated_tm = nc.dram_tensor("gated_tm", [T, HID], BF16).ap()
    attnd_tm = nc.dram_tensor("attnd_tm", [T, HID], BF16).ap()
    h_d = nc.dram_tensor("h_d", [II, 1024], BF16).ap()

    dbg = {}
    if debug_stop == "ln":
        dbg["xn"] = nc.dram_tensor("dbg_xn", [XPAD, HID], F32, kind="ExternalOutput").ap()
    if debug_stop == "proj":
        for nmm, ln_ in (("kT", TE), ("vT", TE), ("rT", T), ("gT", T)):
            dbg[nmm] = nc.dram_tensor("dbg_" + nmm, [HID, ln_], F32, kind="ExternalOutput").ap()
    if debug_stop == "scan":
        dbg["gated"] = nc.dram_tensor("dbg_gated", [T, HID], F32, kind="ExternalOutput").ap()
    if debug_stop == "attnd":
        dbg["attnd"] = nc.dram_tensor("dbg_attnd", [T, HID], F32, kind="ExternalOutput").ap()

    es = ExitStack()
    with es:
        sbt = lambda n_, s_, d_: es.enter_context(nc.sbuf_tensor(n_, s_, d_))
        pst = lambda n_, s_: es.enter_context(nc.psum_tensor(n_, s_, F32))
        c_mask = sbt("c_mask", [128, 1], F32)
        c_mx = sbt("c_mx", [128, 4 * FC], F32)
        c_L = sbt("c_L", [128, FC], F32)
        c_u = sbt("c_u", [128, FC], F32)
        c_ones = sbt("c_ones", [128, 1], BF16)
        c_eps = sbt("c_eps", [128, 1], F32)
        c_onesrow = sbt("c_onesrow", [1, 128], F32)
        c_zero = sbt("c_zero", [128, 128], BF16)
        c_selmask = sbt("c_selmask", [128, 128], U8)
        c_pat = sbt("c_pat", [128, 1536], F32)
        A1 = sbt("A1", [128, 49152], BF16)
        A2 = sbt("A2", [128, 24576], BF16)
        A3 = sbt("A3", [128, 8192], F32)
        pA = pst("pA", [128, 1536])
        pB = pst("pB", [128, 1536])
        pC = pst("pC", [128, 1024])
        _s = {k: es.enter_context(nc.semaphore("s_" + k)) for k in SEMS}
        block = es.enter_context(nc.Block())

        sm = {k: Sem(_s[k]) for k in SEMS}
        gd, xs, ps, gs, as6, od = sm["gd"], sm["xs"], sm["ps"], sm["gs"], sm["as6"], sm["od"]
        ve, ae, pe, ge = sm["ve"], sm["ae"], sm["pe"], sm["ge"]
        tdp = (sm["td0"], sm["td1"])
        ssp = (sm["ss0"], sm["ss1"])
        wdp = (sm["wd0"], sm["wd1"])
        allsems = tuple(sm.values())

        plan = {e: [] for e in ("sync", "scalar", "vector", "tensor", "gpsimd")}

        def op(eng, fn, waits=(), inc=None, dma=False):
            waits = [(w[0], w[1]) for w in waits]
            if not dma:
                if eng == "vector":
                    assert inc is None or inc[0] is ve
                    waits.append((ve, ve.n))
                    inc = (ve, 1)
                elif eng == "scalar":
                    assert inc is None or inc[0] is ae
                    waits.append((ae, ae.n))
                    inc = (ae, 1)
                elif eng == "gpsimd" and inc is not None and inc[0] is ge:
                    waits.append((ge, ge.n))
            if inc is not None:
                inc[0].inc(inc[1])
            plan[eng].append((fn, waits, inc))

        def W(*sems_):
            return [(s_, s_.n) for s_ in sems_]

        op("gpsimd", lambda e: e.dma_start(out=c_mask[:, :], in_=mask_in), inc=(gd, 16))
        for i, m in enumerate(("k", "v", "r", "g")):
            op("gpsimd", lambda e, i=i, m=m: e.dma_start(out=c_mx[:, i * FC:(i + 1) * FC], in_=mx_in[m]), inc=(gd, 16))
        op("gpsimd", lambda e: e.dma_start(out=c_L[:, :], in_=tdec), inc=(gd, 16))
        op("gpsimd", lambda e: e.dma_start(out=c_u[:, :], in_=tfaa), inc=(gd, 16))
        cl = W(gd)
        op("scalar", lambda e: e.activation(c_L[:, :], c_L[:, :], AF.Exp), waits=cl, inc=(ae, 1))
        op("vector", lambda e: e.tensor_scalar_mul(c_L[:, :], c_L[:, :], -1.0), waits=W(ae), inc=(ve, 1))
        op("vector", lambda e: e.memset(c_onesrow[0:1, :], 1.0), inc=(ve, 1))
        op("vector", lambda e: e.memset(c_ones[:, :], 1.0), inc=(ve, 1))
        op("vector", lambda e: e.memset(c_eps[:, :], EPS), inc=(ve, 1))
        op("vector", lambda e: e.memset(c_zero[:, :], 0.0), inc=(ve, 1))
        i_scr = c_pat[:, 1408:1536]
        op("gpsimd", lambda e: e.iota(i_scr, pattern=[[1, 128]], base=0, channel_multiplier=-1,
                                      allow_small_or_imprecise_dtypes=True), inc=(ge, 1))
        op("vector", lambda e: e.tensor_single_scalar(c_selmask[:, :], i_scr, 0.0, ALU.is_gt),
           waits=W(ge), inc=(ve, 1))
        pats = {
            0: ([[0, 8], [1, 16]], -8, 0),
            1: ([[0, 8], [-1, 16]], 7, 0),
            3: ([[0, 4], [-1, 32]], 15, 0),
            4: ([[0, 4], [1, 32]], -16, 0),
            5: ([[0, 2], [-1, 64]], 31, 0),
            6: ([[0, 2], [1, 64]], -32, 0),
            7: ([[3000, 2], [-1, 64]], 63, 0),
            8: ([[-3000, 2], [1, 64]], 3000, 0),
            9: ([[1, 128]], 0, 0),
            10: ([[0, 128]], 127, -1),
        }
        def pat(s_):
            return c_pat[:, s_ * 128:(s_ + 1) * 128]
        for s_, (pp_, bb_, cm_) in pats.items():
            op("gpsimd", lambda e, s_=s_, pp_=pp_, bb_=bb_, cm_=cm_:
               e.iota(pat(s_), pattern=pp_, base=bb_, channel_multiplier=cm_,
                      allow_small_or_imprecise_dtypes=True), inc=(ge, 1))
        op("gpsimd", lambda e: e.iota(pat(2), pattern=[[0, 4], [-3000, 2], [0, 16]], base=3000,
                                      channel_multiplier=0, allow_small_or_imprecise_dtypes=True), inc=(ge, 1))
        op("vector", lambda e: e.tensor_add(pat(2), pat(2), pat(1)), waits=W(ge), inc=(ve, 1))
        ph0 = W(ve, ge, ae, gd)

        xa = A3.ap()[:, 0:4096]
        xsq = A3.ap()[:, 4096:8192]
        st_mu = c_pat[:, 1408:1409]
        st_s2 = c_pat[:, 1409:1410]
        st_rs = c_pat[:, 1410:1411]
        st_t1 = c_pat[:, 1411:1412]
        xnst = A2.ap()[:, 0:4096]
        for it in range(XPAD // 128):
            lw = list(ph0) if it == 0 else W(ve, ae)
            op("sync", lambda e, it=it: e.dma_start(out=xa, in_=x_in[it * 128:(it + 1) * 128, :]),
               waits=lw, inc=(sm["xl"], 16))
            ld = W(sm["xl"])
            op("vector", lambda e: e.reduce_sum(st_mu, xa, axis=AX), waits=ld, inc=(ve, 1))
            op("scalar", lambda e: e.activation(xsq, xa, AF.Square), waits=ld, inc=(ae, 1))
            op("vector", lambda e: e.reduce_sum(st_s2, xsq, axis=AX), waits=W(ae), inc=(ve, 1))
            op("vector", lambda e: e.tensor_scalar_mul(st_mu, st_mu, 1.0 / HID), inc=(ve, 1))
            op("vector", lambda e: e.tensor_scalar_mul(st_s2, st_s2, 1.0 / HID), inc=(ve, 1))
            op("vector", lambda e: e.tensor_tensor(st_t1, st_mu, st_mu, op=ALU.mult), inc=(ve, 1))
            op("vector", lambda e: e.tensor_sub(st_s2, st_s2, st_t1), inc=(ve, 1))
            op("scalar", lambda e: e.activation(st_s2, st_s2, AF.Sqrt, bias=c_eps[:, 0:1]), waits=W(ve), inc=(ae, 1))
            op("vector", lambda e: e.reciprocal(st_rs, st_s2), waits=W(ae), inc=(ve, 1))
            op("vector", lambda e: e.tensor_scalar(xnst, xa, st_mu, st_rs, ALU.subtract, ALU.mult),
               waits=W(xs), inc=(ve, 1))
            op("sync", lambda e, it=it: e.dma_start(out=xn_tm[it * 128:(it + 1) * 128, :], in_=xnst),
               waits=W(ve), inc=(xs, 16))
        op("sync", lambda e: e.dma_start(out=xnst[0:1, :], in_=xn_tm[256:257, :]), waits=W(xs), inc=(sm["xl"], 16))
        op("vector", lambda e: e.tensor_scalar_mul(xnst[0:1, :], xnst[0:1, :], c_mask[0:1, 0:1]),
           waits=W(sm["xl"]), inc=(ve, 1))
        op("sync", lambda e: e.dma_start(out=xn_tm[256:257, :], in_=xnst[0:1, :]), waits=W(ve), inc=(xs, 16))
        xn_ready = W(xs)

        if debug_stop == "ln":
            for it in range(XPAD // 128):
                op("gpsimd", lambda e, it=it: e.dma_start(out=dbg["xn"][it * 128:(it + 1) * 128, :],
                                                          in_=xn_tm[it * 128:(it + 1) * 128, :]),
                   waits=xn_ready, inc=(gd, 16))
            _emit(block, plan, allsems)
            return nc

        mixT = A1.ap()[:, 0:FC * TE]
        def mix_sl(j, t0, tl):
            return mixT[:, j * TE + t0: j * TE + t0 + tl]
        xnc = [A2.ap()[:, 0:1408], A2.ap()[:, 1408:2816]]
        delta = A2.ap()[:, 2816:4224]
        slab = [A2.ap()[:, 4224 + i * 1024:4224 + (i + 1) * 1024] for i in range(2)]
        evst = A2.ap()[:, 6272:6272 + 1280]
        mix_idx = {"k": 0, "v": 1, "r": 2, "g": 3}
        projs = [("k", TE, 0, kT_d), ("v", TE, 0, vT_d), ("r", T, BT, rT_d), ("g", T, BT, gT_d)]
        slab_last_use = [W(pe), W(pe)]
        for pi, (m, TL, toff, dst_d) in enumerate(projs):
            for j in range(FC):
                bsl = xnc[j % 2]
                tsem = tdp[j % 2]
                ww = (xn_ready if (pi == 0 and j < 2) else W(ve))
                op("scalar", lambda e, j=j, bsl=bsl: e.dma_start_transpose(bsl, xn_tm[:, j * 128:(j + 1) * 128]),
                   waits=ww, inc=(tsem, 16), dma=True)
                op("vector", lambda e, bsl=bsl, toff=toff, TL=TL:
                   e.tensor_sub(delta[:, 0:TL], bsl[:, 1 + toff:1 + toff + TL], bsl[:, toff:toff + TL]),
                   waits=W(tsem), inc=(ve, 1))
                op("vector", lambda e, j=j, m=m, bsl=bsl, toff=toff, TL=TL:
                   e.scalar_tensor_tensor(mix_sl(j, 0, TL), delta[:, 0:TL],
                                          c_mx[:, mix_idx[m] * FC + j: mix_idx[m] * FC + j + 1],
                                          bsl[:, toff:toff + TL], ALU.mult, ALU.add), inc=(ve, 1))
            mix_done = W(ve)
            for imac in range(HID // 256):
                i0 = imac * 256
                for kk in range(FC):
                    bpar = (kk // 4) % 2
                    bsl = slab[bpar]
                    wsem = wdp[bpar]
                    if kk % 4 == 0:
                        ww = list(slab_last_use[bpar]) + (list(mix_done) if (imac == 0 and kk < 8) else [])
                        op("gpsimd", lambda e, m=m, kk=kk, i0=i0, bsl=bsl:
                           e.dma_start(out=bsl[:, 0:1024].rearrange("p (a i) -> p a i", a=4),
                                       in_=wT[m][kk * 128:(kk + 4) * 128, i0:i0 + 256]
                                       .rearrange("(a p) i -> p a i", p=128)),
                           waits=ww, inc=(wsem, 16))
                    for q in range(2):
                        pt = (pA, pB)[q]
                        for nn in range((TL + 511) // 512):
                            n0, n1 = nn * 512, min((nn + 1) * 512, TL)
                            op("tensor", lambda e, kk=kk, q=q, pt=pt, n0=n0, n1=n1, bsl=bsl:
                               e.matmul(pt[:, n0:n1],
                                        bsl[:, (kk % 4) * 256 + q * 128:(kk % 4) * 256 + (q + 1) * 128],
                                        mix_sl(kk, n0, n1 - n0),
                                        start=(kk == 0), stop=(kk == FC - 1)),
                               waits=W(wsem) + (W(ve) if kk == 0 else []), inc=(pe, 1))
                    if kk % 4 == 3:
                        slab_last_use[bpar] = W(pe)
                for q in range(2):
                    pt = (pA, pB)[q]
                    op("vector", lambda e, pt=pt, TL=TL: e.tensor_copy(evst[:, 0:TL], pt[:, 0:TL]),
                       waits=W(pe, ps), inc=(ve, 1))
                    op("sync", lambda e, i0=i0, q=q, dst_d=dst_d, TL=TL:
                       e.dma_start(out=dst_d[i0 + q * 128:i0 + (q + 1) * 128, 0:TL], in_=evst[:, 0:TL]),
                       waits=W(ve), inc=(ps, 16))
        proj_done = W(ps)

        if debug_stop == "proj":
            for nmm, src_d, TL in (("kT", kT_d, TE), ("vT", vT_d, TE), ("rT", rT_d, T), ("gT", gT_d, T)):
                for it in range(FC):
                    op("gpsimd", lambda e, nmm=nmm, src_d=src_d, it=it, TL=TL:
                       e.dma_start(out=dbg[nmm][it * 128:(it + 1) * 128, :], in_=src_d[it * 128:(it + 1) * 128, :]),
                       waits=proj_done, inc=(gd, 16))
            _emit(block, plan, allsems)
            return nc

        Lrow = A3.ap()[:, 0:4096]
        xrow = A3.ap()[0:1, 4096:8192]
        Sst = A3.ap()[0:64, 4224:4352]
        escr = A3.ap()[:, 4352:4480]
        GNS = A3.ap()[:, 4480:4608]
        ebig = A3.ap()[:, 4608:6016]
        gsb = A1.ap()[:, 0:32768]
        gnw_t = A1.ap()[:, 32768:36864]
        gnb_t = A1.ap()[:, 36864:40960]
        pwa = [A1.ap()[:, 40960 + s * 1408: 40960 + (s + 1) * 1408] for s in range(2)]
        def pwt(s, i):
            return pwa[s][:, i * 128:(i + 1) * 128]
        def a2v(i_):
            return A2.ap()[:, i_ * 128:(i_ + 1) * 128]
        rfb = [a2v(0), a2v(1)]
        kfb = [a2v(2), a2v(3)]
        ktb = [a2v(4), a2v(5)]
        vtb = [a2v(6), a2v(7)]
        gtb = [a2v(8), a2v(9)]
        stt = [[a2v(10 + p * 14 + i) for i in range(14)] for p in range(2)]

        op("gpsimd", lambda e: e.dma_start(out=xrow[0:1, 0:HID], in_=tdecr), waits=proj_done, inc=(gd, 16))
        op("scalar", lambda e: e.activation(xrow[0:1, 0:HID], xrow[0:1, 0:HID], AF.Exp), waits=W(gd), inc=(ae, 1))
        op("vector", lambda e: e.tensor_scalar_mul(xrow[0:1, 0:HID], xrow[0:1, 0:HID], -1.0), waits=W(ae), inc=(ve, 1))
        for q in range(8):
            op("tensor", lambda e, q=q: e.matmul(pC[:, 0:512], c_onesrow[0:1, :], xrow[0:1, q * 512:(q + 1) * 512],
                                                 start=True, stop=True), waits=W(ve), inc=(pe, 1))
            op("vector", lambda e, q=q: e.tensor_copy(Lrow[:, q * 512:(q + 1) * 512], pC[:, 0:512]),
               waits=W(pe), inc=(ve, 1))
        for (dst_t, src_r) in ((gnw_t, gnw_in), (gnb_t, gnb_in)):
            op("gpsimd", lambda e, src_r=src_r: e.dma_start(out=xrow[0:1, 0:HID], in_=src_r), waits=W(ve), inc=(gd, 16))
            for q in range(8):
                op("tensor", lambda e, q=q: e.matmul(pC[:, 0:512], c_onesrow[0:1, :], xrow[0:1, q * 512:(q + 1) * 512],
                                                     start=True, stop=True), waits=W(gd, ve), inc=(pe, 1))
                op("vector", lambda e, dst_t=dst_t, q=q: e.tensor_copy(dst_t[:, q * 512:(q + 1) * 512], pC[:, 0:512]),
                   waits=W(pe), inc=(ve, 1))
        scan_pre = W(ve, pe, ae, gd)

        def pset(par):
            P = (pA, pB)[par]
            return {"AA": P[:, 0:128], "AB": P[:, 128:256],
                    "OA": P[:, 512:576], "OB": P[:, 576:640],
                    "cA": P[:, 640:641], "cB": P[:, 641:642],
                    "SA": P[0:64, 1024:1088], "SB": P[0:64, 1088:1152]}

        NI = NPAIR * NCH

        def inst(k):
            return (k // NCH, k % NCH) if k is not None and 0 <= k < NI else (None, None)

        ld_last = [scan_pre, scan_pre]
        pw_last = [scan_pre, scan_pre]
        stage_marks = {}

        def em_loads(k):
            mpr, n = inst(k)
            if mpr is None:
                return
            cpar = k % 2
            rf, kf, kt, vt, gt = rfb[cpar], kfb[cpar], ktb[cpar], vtb[cpar], gtb[cpar]
            tsem, ssem = tdp[cpar], ssp[cpar]
            t0e = n * CH
            boot = n < 2
            ww = list(ld_last[cpar])
            op("scalar", lambda e, mpr=mpr, t0e=t0e, kt=kt:
               e.dma_start_transpose(kt, kT_d[mpr * 128:(mpr + 1) * 128, t0e:t0e + CH]), waits=ww, inc=(tsem, 16), dma=True)
            op("scalar", lambda e, mpr=mpr, t0e=t0e, vt=vt:
               e.dma_start_transpose(vt, vT_d[mpr * 128:(mpr + 1) * 128, t0e:t0e + CH]), inc=(tsem, 16), dma=True)
            op("sync", lambda e, mpr=mpr, t0e=t0e, kf=kf:
               e.dma_start(out=kf, in_=kT_d[mpr * 128:(mpr + 1) * 128, t0e:t0e + CH]), waits=ww, inc=(ssem, 16))
            if not boot:
                top = t0e - BT
                op("scalar", lambda e, mpr=mpr, top=top, gt=gt:
                   e.dma_start_transpose(gt, gT_d[mpr * 128:(mpr + 1) * 128, top:top + CH]), inc=(tsem, 16), dma=True)
                op("sync", lambda e, mpr=mpr, top=top, rf=rf:
                   e.dma_start(out=rf, in_=rT_d[mpr * 128:(mpr + 1) * 128, top:top + CH]), inc=(ssem, 16))
            stage_marks[("ld", k)] = W(tsem, ssem)

        def em_powers(mpr):
            ppar = mpr % 2
            Lcol = c_L[:, mpr:mpr + 1]
            ww = list(pw_last[ppar])
            for s_ in range(10):
                op("vector", lambda e, s_=s_, Lcol=Lcol:
                   e.tensor_scalar(ebig[:, s_ * 128:(s_ + 1) * 128], pat(s_), Lcol, CLAMP, ALU.mult, ALU.min),
                   waits=(ww if s_ == 0 else []), inc=(ve, 1))
            op("vector", lambda e, mpr=mpr:
               e.tensor_tensor(ebig[:, 1280:1408], pat(10), Lrow[:, mpr * 128:(mpr + 1) * 128], op=ALU.mult),
               inc=(ve, 1))
            op("scalar", lambda e, ppar=ppar: e.activation(pwa[ppar][:, 0:1408], ebig[:, 0:1408], AF.Exp),
               waits=W(ve), inc=(ae, 1))
            op("vector", lambda e, Lcol=Lcol, ppar=ppar:
               e.tensor_single_scalar(GNS[:, 8 + 2 * ppar:9 + 2 * ppar], Lcol, 128.0, ALU.mult), inc=(ve, 1))
            op("scalar", lambda e, ppar=ppar: e.activation(GNS[:, 8 + 2 * ppar:9 + 2 * ppar],
                                                           GNS[:, 8 + 2 * ppar:9 + 2 * ppar], AF.Exp),
               waits=W(ve), inc=(ae, 1))
            op("vector", lambda e, ppar=ppar: e.tensor_copy(GNS[0:64, 9 + 2 * ppar:10 + 2 * ppar],
                                                            GNS[64:128, 8 + 2 * ppar:9 + 2 * ppar]),
               waits=W(ae), inc=(ve, 1))
            stage_marks[("pw", mpr)] = W(ae, ve)

        def em_scales(k):
            mpr, n = inst(k)
            if mpr is None:
                return
            if n == 0:
                em_powers(mpr)
                op("vector", lambda e: e.memset(Sst[:, :], 0.0), inc=(ve, 1))
            cpar = k % 2
            ppar = mpr % 2
            boot = n < 2
            st = stt[cpar]
            rf, kf, kt = rfb[cpar], kfb[cpar], ktb[cpar]
            gt = gtb[cpar]
            ucol = c_u[:, mpr:mpr + 1]
            ww = list(stage_marks[("ld", k)]) + list(stage_marks[("pw", mpr)])
            if boot:
                op("vector", lambda e, kt=kt: e.tensor_scalar_mul(kt, kt, c_mask[:, 0:1]), waits=ww, inc=(ve, 1))
                op("vector", lambda e, kt=kt, st=st, ppar=ppar: e.tensor_tensor(st[0], kt, pwt(ppar, 10), op=ALU.mult),
                   inc=(ve, 1))
            else:
                op("scalar", lambda e, gt=gt: e.activation(gt, gt, AF.Silu), waits=ww, inc=(ae, 1))
                op("vector", lambda e, kt=kt, st=st, ppar=ppar: e.tensor_tensor(st[0], kt, pwt(ppar, 10), op=ALU.mult),
                   waits=ww, inc=(ve, 1))
                op("vector", lambda e, rf=rf, st=st, ppar=ppar: e.tensor_tensor(st[1], rf, pwt(ppar, 0), op=ALU.mult), inc=(ve, 1))
                op("vector", lambda e, kf=kf, st=st, ppar=ppar: e.tensor_tensor(st[2], kf, pwt(ppar, 1), op=ALU.mult), inc=(ve, 1))
                op("vector", lambda e, kf=kf, st=st, ppar=ppar: e.tensor_tensor(st[3], kf, pwt(ppar, 2), op=ALU.mult), inc=(ve, 1))
                op("vector", lambda e, kf=kf, st=st, ppar=ppar: e.tensor_tensor(st[4], kf, pwt(ppar, 3), op=ALU.mult), inc=(ve, 1))
                op("vector", lambda e, rf=rf, st=st, ppar=ppar: e.tensor_tensor(st[5], rf, pwt(ppar, 4), op=ALU.mult), inc=(ve, 1))
                op("vector", lambda e, kf=kf, st=st, ppar=ppar: e.tensor_tensor(st[6], kf, pwt(ppar, 5), op=ALU.mult), inc=(ve, 1))
                op("vector", lambda e, rf=rf, st=st, ppar=ppar: e.tensor_tensor(st[7], rf, pwt(ppar, 6), op=ALU.mult), inc=(ve, 1))
                op("vector", lambda e, kf=kf, st=st, ppar=ppar: e.tensor_tensor(st[8], kf, pwt(ppar, 7), op=ALU.mult), inc=(ve, 1))
                op("vector", lambda e, rf=rf, st=st, ppar=ppar: e.tensor_tensor(st[9], rf, pwt(ppar, 8), op=ALU.mult), inc=(ve, 1))
                op("vector", lambda e, rf=rf, st=st, ppar=ppar: e.tensor_tensor(st[10][0:64, :], rf[0:64, :], pwt(ppar, 9)[0:64, :], op=ALU.mult), inc=(ve, 1))
                op("vector", lambda e, rf=rf, st=st, ppar=ppar: e.tensor_tensor(st[11][0:64, :], rf[64:128, :], pwt(ppar, 9)[64:128, :], op=ALU.mult), inc=(ve, 1))
                op("vector", lambda e, rf=rf, st=st, ucol=ucol: e.tensor_scalar_mul(st[12], rf, ucol), inc=(ve, 1))
                op("vector", lambda e, kf=kf, st=st: e.tensor_tensor(st[12], st[12], kf, op=ALU.mult), inc=(ve, 1))
            stage_marks[("sc", k)] = W(ve)

        def em_abuild(k):
            mpr, n = inst(k)
            if mpr is None:
                return
            if n < 2:
                stage_marks[("ab", k)] = []
                return
            cpar = k % 2
            PS = pset(cpar)
            st = stt[cpar]
            ww = list(stage_marks[("sc", k)])
            for hh in range(2):
                dps = PS["AA"] if hh == 0 else PS["AB"]
                rb = slice(hh * 64, hh * 64 + 64)
                tp = hh * 64
                op("tensor", lambda e, dps=dps, rb=rb, tp=tp, st=st:
                   e.matmul(dps[0:128, 0:128], st[8][rb, :], st[9][rb, :],
                            start=True, stop=True, tile_position=(tp, 0), skip_group_check=True),
                   waits=(ww if hh == 0 else []), inc=(pe, 1))
                def AMM(lhs_i, l0, l1, r0_, r1_, o0, o1, t0_, t1_):
                    op("tensor", lambda e, lhs_i=lhs_i, l0=l0, l1=l1, r0_=r0_, r1_=r1_, o0=o0, o1=o1,
                       t0_=t0_, t1_=t1_, dps=dps, rb=rb, tp=tp, st=st:
                       e.matmul(dps[o0:o1, t0_:t1_], st[lhs_i][rb, l0:l1], st[1][rb, r0_:r1_],
                                start=False, stop=True, tile_position=(tp, o0 - o0 % 32),
                                skip_group_check=True), inc=(pe, 1))
                for kb in range(4):
                    b0 = kb * 32
                    AMM(2, b0, b0 + 16, b0, b0 + 16, b0, b0 + 16, b0, b0 + 16)
                    AMM(3, b0, b0 + 32, b0 + 16, b0 + 32, b0, b0 + 32, b0 + 16, b0 + 32)
                    op("tensor", lambda e, b0=b0, dps=dps, rb=rb, tp=tp, st=st:
                       e.matmul(dps[b0:b0 + 16, b0 + 16:b0 + 32], st[4][rb, b0:b0 + 16],
                                st[5][rb, b0 + 16:b0 + 32], start=False, stop=True,
                                tile_position=(tp, b0), skip_group_check=True), inc=(pe, 1))
                for kb in range(2):
                    b0 = kb * 64
                    op("tensor", lambda e, b0=b0, dps=dps, rb=rb, tp=tp, st=st:
                       e.matmul(dps[b0:b0 + 32, b0 + 32:b0 + 64], st[6][rb, b0:b0 + 32],
                                st[7][rb, b0 + 32:b0 + 64], start=False, stop=True,
                                tile_position=(tp, b0), skip_group_check=True), inc=(pe, 1))
                op("tensor", lambda e, hh=hh, rb=rb, tp=tp, st=st, PS=PS:
                   e.matmul((PS["cA"] if hh == 0 else PS["cB"]), st[12][rb, :], c_ones[rb, 0:1],
                            start=True, stop=True, tile_position=(tp, 0), skip_group_check=True),
                   inc=(pe, 1))
            stage_marks[("ab", k)] = W(pe)

        def em_selects(k):
            mpr, n = inst(k)
            if mpr is None:
                return
            if n < 2:
                stage_marks[("sel", k)] = []
                return
            cpar = k % 2
            PS = pset(cpar)
            st = stt[cpar]
            ww = list(stage_marks[("ab", k)])
            op("vector", lambda e, st=st, PS=PS: e.select(st[1], c_selmask[:, :], PS["AA"], c_zero[:, :], add_drain=True),
               waits=ww, inc=(ve, 1))
            op("vector", lambda e, st=st, PS=PS: e.select(st[2], c_selmask[:, :], PS["AB"], c_zero[:, :], add_drain=True),
               inc=(ve, 1))
            op("vector", lambda e, PS=PS, cpar=cpar: e.tensor_copy(escr[:, 2 * cpar:2 * cpar + 1], PS["cA"]), inc=(ve, 1))
            op("vector", lambda e, PS=PS, cpar=cpar: e.tensor_copy(escr[:, 2 * cpar + 1:2 * cpar + 2], PS["cB"]), inc=(ve, 1))
            op("vector", lambda e, st=st: e.tensor_copy(st[13][0:64, :], Sst[0:64, :]), inc=(ve, 1))
            stage_marks[("sel", k)] = W(ve)

        def em_mm2(k):
            mpr, n = inst(k)
            if mpr is None:
                return
            cpar = k % 2
            PS = pset(cpar)
            st = stt[cpar]
            vt = vtb[cpar]
            boot = n < 2
            if not boot:
                ww = list(stage_marks[("sel", k)])
                for hh in range(2):
                    asb = st[1] if hh == 0 else st[2]
                    rc = st[10] if hh == 0 else st[11]
                    dpo = PS["OA"] if hh == 0 else PS["OB"]
                    op("tensor", lambda e, asb=asb, dpo=dpo, hh=hh, vt=vt:
                       e.matmul(dpo[:, 0:64], asb, vt[:, hh * 64:hh * 64 + 64],
                                start=True, stop=False, skip_group_check=True),
                       waits=(ww if hh == 0 else []), inc=(pe, 1))
                    op("tensor", lambda e, rc=rc, dpo=dpo, hh=hh, st=st:
                       e.matmul(dpo[:, 0:64], rc[0:64, :], st[13][0:64, hh * 64:hh * 64 + 64],
                                start=False, stop=True, skip_group_check=True), inc=(pe, 1))
            sw = list(stage_marks[("sc", k)]) if boot else []
            for hh in range(2):
                op("tensor", lambda e, hh=hh, vt=vt, st=st, PS=PS:
                   e.matmul((PS["SA"] if hh == 0 else PS["SB"]), st[0][:, hh * 64:hh * 64 + 64],
                            vt[:, hh * 64:hh * 64 + 64], start=True, stop=True, skip_group_check=True),
                   waits=(sw if hh == 0 else []), inc=(pe, 1))
            stage_marks[("mm2", k)] = W(pe)

        def em_epi(k):
            mpr, n = inst(k)
            if mpr is None:
                return
            cpar = k % 2
            ppar = mpr % 2
            PS = pset(cpar)
            st = stt[cpar]
            vt = vtb[cpar]
            gt = gtb[cpar]
            boot = n < 2
            smm = list(stage_marks[("mm2", k)])
            if not boot:
                for hh in range(2):
                    op("vector", lambda e, hh=hh, vt=vt, st=st, PS=PS, cpar=cpar:
                       e.scalar_tensor_tensor(st[3][:, hh * 64:hh * 64 + 64], vt[:, hh * 64:hh * 64 + 64],
                                              escr[:, 2 * cpar + hh:2 * cpar + hh + 1],
                                              (PS["OA"] if hh == 0 else PS["OB"])[:, 0:64],
                                              ALU.mult, ALU.add), waits=(smm if hh == 0 else []), inc=(ve, 1))
                op("vector", lambda e, st=st: e.reduce_sum(GNS[:, 0:2], st[3].rearrange("p (s d) -> p s d", s=2), axis=AX),
                   inc=(ve, 1))
                op("scalar", lambda e, st=st: e.activation(st[4], st[3], AF.Square), waits=W(ve), inc=(ae, 1))
                op("vector", lambda e, st=st: e.reduce_sum(GNS[:, 2:4], st[4].rearrange("p (s d) -> p s d", s=2), axis=AX),
                   waits=W(ae), inc=(ve, 1))
                op("vector", lambda e: e.tensor_scalar_mul(GNS[:, 0:2], GNS[:, 0:2], 1.0 / 64), inc=(ve, 1))
                op("vector", lambda e: e.tensor_scalar_mul(GNS[:, 2:4], GNS[:, 2:4], 1.0 / 64), inc=(ve, 1))
                op("vector", lambda e: e.tensor_tensor(GNS[:, 4:6], GNS[:, 0:2], GNS[:, 0:2], op=ALU.mult), inc=(ve, 1))
                op("vector", lambda e: e.tensor_sub(GNS[:, 2:4], GNS[:, 2:4], GNS[:, 4:6]), inc=(ve, 1))
                op("scalar", lambda e: e.activation(GNS[:, 2:4], GNS[:, 2:4], AF.Sqrt, bias=c_eps[:, 0:1]),
                   waits=W(ve), inc=(ae, 1))
                op("vector", lambda e: e.reciprocal(GNS[:, 2:4], GNS[:, 2:4]), waits=W(ae), inc=(ve, 1))
                for hh in range(2):
                    op("vector", lambda e, hh=hh, st=st:
                       e.tensor_scalar(st[3][:, hh * 64:hh * 64 + 64], st[3][:, hh * 64:hh * 64 + 64],
                                       GNS[:, hh:hh + 1], GNS[:, 2 + hh:3 + hh], ALU.subtract, ALU.mult),
                       inc=(ve, 1))
                op("vector", lambda e, mpr=mpr, st=st: e.tensor_tensor(st[3], st[3], gnw_t[:, mpr * 128:(mpr + 1) * 128],
                                                                       op=ALU.mult), inc=(ve, 1))
                op("vector", lambda e, mpr=mpr, st=st: e.tensor_add(st[3], st[3], gnb_t[:, mpr * 128:(mpr + 1) * 128]),
                   inc=(ve, 1))
                op("vector", lambda e, mpr=mpr, n=n, gt=gt, st=st:
                   e.tensor_tensor(gsb[:, (n - 2) * 4096 + mpr * 128:(n - 2) * 4096 + (mpr + 1) * 128],
                                   st[3], gt, op=ALU.mult), waits=W(ae), inc=(ve, 1))
            if n < NCH - 1:
                for hh in range(2):
                    op("vector", lambda e, hh=hh, PS=PS, ppar=ppar:
                       e.scalar_tensor_tensor(Sst[0:64, hh * 64:hh * 64 + 64], Sst[0:64, hh * 64:hh * 64 + 64],
                                              GNS[0:64, 8 + 2 * ppar + hh:9 + 2 * ppar + hh],
                                              (PS["SA"] if hh == 0 else PS["SB"]), ALU.mult, ALU.add),
                       waits=(smm if boot else []), inc=(ve, 1))
            stage_marks[("epi", k)] = W(ve, ae)
            ld_last[cpar] = W(ve, ae, pe)
            if n == NCH - 1:
                pw_last[mpr % 2] = W(ve, ae, pe)

        em_loads(0)
        em_loads(1)
        em_scales(0)
        for k in range(NI):
            em_abuild(k)
            em_selects(k)
            em_mm2(k)
            em_scales(k + 1)
            em_epi(k)
            em_loads(k + 2)
        scan_ve = W(ve)
        for cth in range(8):
            op("sync", lambda e, cth=cth: e.dma_start(out=gated_tm[cth * 128:(cth + 1) * 128, :],
                                                      in_=gsb[:, cth * 4096:(cth + 1) * 4096]),
               waits=scan_ve, inc=(gs, 16))
        scan_done = W(gs)

        if debug_stop == "scan":
            for it in range(T // 128):
                op("gpsimd", lambda e, it=it: e.dma_start(out=dbg["gated"][it * 128:(it + 1) * 128, :],
                                                          in_=gated_tm[it * 128:(it + 1) * 128, :]),
                   waits=scan_done, inc=(gd, 16))
            _emit(block, plan, allsems)
            return nc

        gatT = A1.ap()[:, 0:FC * T]
        def gat_sl(j, t0, tl):
            return gatT[:, j * T + t0:j * T + t0 + tl]
        for j in range(FC):
            op("scalar", lambda e, j=j: e.dma_start_transpose(gatT[:, j * T:(j + 1) * T],
                                                              gated_tm[:, j * 128:(j + 1) * 128]),
               waits=scan_done, inc=(tdp[0], 16), dma=True)
        gat_rdy = W(tdp[0])
        oslab = [A2.ap()[:, i * 512:(i + 1) * 512] for i in range(2)]
        ost = A2.ap()[:, 2048:2048 + 4096]
        dpt8 = [pA[:, 0:512], pA[:, 512:1024], pA[:, 1024:1536],
                pB[:, 0:512], pB[:, 512:1024], pB[:, 1024:1536],
                pC[:, 0:512], pC[:, 512:1024]]
        oslab_last = [gat_rdy, gat_rdy]
        for iqh in range(8):
            i0 = iqh * 512
            for kk in range(FC):
                bpar = kk % 2
                bsl = oslab[bpar]
                wsem = wdp[bpar]
                ww = list(oslab_last[bpar]) + (list(gat_rdy) if (iqh == 0 and kk < 2) else [])
                op("gpsimd", lambda e, kk=kk, i0=i0, bsl=bsl:
                   e.dma_start(out=bsl[:, 0:512], in_=wT["o"][kk * 128:(kk + 1) * 128, i0:i0 + 512]),
                   waits=ww, inc=(wsem, 16))
                for tt in range(8):
                    op("tensor", lambda e, tt=tt, kk=kk, bsl=bsl:
                       e.matmul(dpt8[tt], gat_sl(kk, tt * 128, 128), bsl[:, 0:512],
                                start=(kk == 0), stop=(kk == FC - 1)),
                       waits=W(wsem) + (W(ve) if kk == 0 else []), inc=(pe, 1))
                oslab_last[bpar] = W(pe)
            for tt in range(8):
                op("vector", lambda e, tt=tt: e.tensor_copy(ost[:, tt * 512:(tt + 1) * 512], dpt8[tt]),
                   waits=W(pe, as6), inc=(ve, 1))
            for tt in range(8):
                op("sync", lambda e, tt=tt, i0=i0: e.dma_start(out=attnd_tm[tt * 128:(tt + 1) * 128, i0:i0 + 512],
                                                               in_=ost[:, tt * 512:(tt + 1) * 512]),
                   waits=W(ve), inc=(as6, 16))
        attnd_done = W(as6)

        if debug_stop == "attnd":
            for it in range(T // 128):
                op("gpsimd", lambda e, it=it: e.dma_start(out=dbg["attnd"][it * 128:(it + 1) * 128, :],
                                                          in_=attnd_tm[it * 128:(it + 1) * 128, :]),
                   waits=attnd_done, inc=(gd, 16))
            _emit(block, plan, allsems)
            return nc

        xnT6 = A1.ap()[:, 0:FC * 1024]
        fslab = [A2.ap()[:, i * 1024:(i + 1) * 1024] for i in range(4)]
        dslab = [A2.ap()[:, 4096 + i * 2048:4096 + (i + 1) * 2048] for i in range(2)]
        hlt = [A2.ap()[:, 8192 + i * 1024:8192 + (i + 1) * 1024] for i in range(2)]
        fst = A2.ap()[:, 10240:10240 + 1024]
        hst = A2.ap()[:, 11264:11264 + 1024]
        xres = A3.ap()[:, 0:2048]
        sres = A3.ap()[:, 2048:4096]
        for j in range(FC):
            op("scalar", lambda e, j=j: e.dma_start_transpose(xnT6[:, j * 1024:(j + 1) * 1024],
                                                              xn_tm[257:1281, j * 128:(j + 1) * 128]),
               waits=(attnd_done if j < 2 else W(pe, ve)), inc=(tdp[0], 16), dma=True)
        xn_rdy = W(tdp[0])
        fslab_last = [xn_rdy, xn_rdy, xn_rdy, xn_rdy]
        fsems = (wdp[0], wdp[1], sm["wd2"], sm["wd3"])
        for ii in range(IC):
            for mg in ("g", "u"):
                wsrc = wgT2 if mg == "g" else wuT2
                ptt = pA if mg == "g" else pB
                for kk in range(FC):
                    kg = kk // 8
                    bi = (0 if mg == "g" else 2) + kg % 2
                    bsl = fslab[bi]
                    wsem = fsems[bi]
                    if kk % 8 == 0:
                        ww = list(fslab_last[bi]) + (list(xn_rdy) if ii == 0 else [])
                        op("gpsimd", lambda e, kk=kk, ii=ii, bsl=bsl, wsrc=wsrc:
                           e.dma_start(out=bsl[:, 0:1024].rearrange("p (a i) -> p a i", a=8),
                                       in_=wsrc[kk * 128:(kk + 8) * 128, ii * 128:(ii + 1) * 128]
                                       .rearrange("(a p) i -> p a i", p=128)),
                           waits=ww, inc=(wsem, 16))
                    for nn in range(2):
                        op("tensor", lambda e, kk=kk, nn=nn, ptt=ptt, bsl=bsl:
                           e.matmul(ptt[:, nn * 512:(nn + 1) * 512], bsl[:, (kk % 8) * 128:(kk % 8 + 1) * 128],
                                    xnT6[:, kk * 1024 + nn * 512:kk * 1024 + (nn + 1) * 512],
                                    start=(kk == 0), stop=(kk == FC - 1)),
                           waits=W(wsem) + (W(ve, ae) if kk == 0 else []), inc=(pe, 1))
                    if kk % 8 == 7:
                        fslab_last[bi] = W(pe)
            op("scalar", lambda e: e.activation(fst[:, 0:1024], pA[:, 0:1024], AF.Silu),
               waits=W(pe), inc=(ae, 1))
            op("vector", lambda e: e.tensor_tensor(hst[:, 0:1024], fst[:, 0:1024], pB[:, 0:1024],
                                                   op=ALU.mult), waits=W(ae, ps), inc=(ve, 1))
            op("sync", lambda e, ii=ii: e.dma_start(out=h_d[ii * 128:(ii + 1) * 128, :], in_=hst[:, 0:1024]),
               waits=W(ve), inc=(ps, 16))
        h_rdy = W(ve, ps)
        dslab_last = [h_rdy, h_rdy]
        hl_last = [h_rdy, h_rdy]
        for ho in range(8):
            hd0 = ho * 512
            for kk in range(IC):
                bpar = (kk // 4) % 2
                bsl = dslab[bpar]
                wsem = wdp[bpar]
                hpar = kk % 2
                ht = hlt[hpar]
                hsem = ssp[hpar]
                if kk % 4 == 0:
                    kh = min(4, IC - kk)
                    ww = list(dslab_last[bpar]) + (list(h_rdy) if (ho == 0 and kk < 8) else [])
                    op("gpsimd", lambda e, kk=kk, kh=kh, hd0=hd0, bsl=bsl:
                       e.dma_start(out=bsl[:, 0:kh * 512].rearrange("p (a h) -> p a h", h=512),
                                   in_=wdT2[kk * 128:(kk + kh) * 128, hd0:hd0 + 512]
                                   .rearrange("(a p) h -> p a h", p=128)),
                       waits=ww, inc=(wsem, 16))
                op("sync", lambda e, kk=kk, ht=ht: e.dma_start(out=ht, in_=h_d[kk * 128:(kk + 1) * 128, :]),
                   waits=list(hl_last[hpar]), inc=(hsem, 16))
                for tt in range(8):
                    op("tensor", lambda e, kk=kk, tt=tt, bsl=bsl, ht=ht:
                       e.matmul(dpt8[tt], ht[:, tt * 128:(tt + 1) * 128],
                                bsl[:, (kk % 4) * 512:(kk % 4 + 1) * 512],
                                start=(kk == 0), stop=(kk == IC - 1)),
                       waits=W(wsem, hsem) + (W(ve) if kk == 0 else []), inc=(pe, 1))
                if kk % 4 == 3 or kk == IC - 1:
                    dslab_last[bpar] = W(pe)
                hl_last[hpar] = W(pe)
            mmd = W(pe)
            for tt in range(8):
                r0 = tt * 128
                op("sync", lambda e, r0=r0, hd0=hd0, tt=tt:
                   e.dma_start(out=xres[:, (tt % 4) * 512:(tt % 4 + 1) * 512],
                               in_=x_in[257 + r0:257 + r0 + 128, hd0:hd0 + 512]),
                   waits=mmd + W(ve), inc=(sm["xl"], 16))
                op("scalar", lambda e, r0=r0, hd0=hd0: e.dma_start(out=fst[:, 0:512],
                                                                   in_=attnd_tm[r0:r0 + 128, hd0:hd0 + 512]),
                   waits=mmd + W(ve), inc=(tdp[1], 16), dma=True)
                op("vector", lambda e, tt=tt: e.tensor_tensor(sres[:, 0:512], xres[:, (tt % 4) * 512:(tt % 4 + 1) * 512],
                                                              fst[:, 0:512], op=ALU.add),
                   waits=W(sm["xl"]) + W(tdp[1]) + W(od), inc=(ve, 1))
                op("vector", lambda e, tt=tt: e.tensor_tensor(sres[:, 512:1024], sres[:, 0:512], dpt8[tt],
                                                              op=ALU.add), inc=(ve, 1))
                op("sync", lambda e, r0=r0, hd0=hd0: e.dma_start(out=y_out[r0:r0 + 128, hd0:hd0 + 512],
                                                                 in_=sres[:, 512:1024]),
                   waits=W(ve), inc=(od, 16))
        _emit(block, plan, allsems)
    return nc


def _emit(block, plan, allsems):
    def tail(e):
        for s_ in allsems:
            if s_.n:
                e.wait_ge(s_.h, s_.n)
        return None
    plan["sync"].append((tail, [], None))

    def runner(name):
        def go(eng):
            for fn, waits, inc in plan[name]:
                for s_, v_ in waits:
                    if v_ > 0:
                        eng.wait_ge(s_.h, v_)
                ins = fn(eng)
                if inc is not None:
                    s_, k_ = inc
                    assert ins is not None, name
                    ins.then_inc(s_.h, k_)
        return go
    block.sync(runner("sync"))
    block.scalar(runner("scalar"))
    block.vector(runner("vector"))
    block.tensor(runner("tensor"))
    block.gpsimd(runner("gpsimd"))


HOST_SHARED = {}


def prep_shared(full):
    t = lambda w: np.ascontiguousarray(np.asarray(w).T)
    HOST_SHARED["wkT"] = t(full["L0_attn_w_key"])
    HOST_SHARED["wvT"] = t(full["L0_attn_w_value"])
    HOST_SHARED["wrT"] = t(full["L0_attn_w_receptance"])
    HOST_SHARED["wgT"] = t(full["L0_attn_w_gate"])
    HOST_SHARED["woT"] = t(full["L0_attn_w_output"])
    cb = np.asarray(full["L0_ffn_codebook"])
    def deq(idx):
        idx = np.asarray(idx)
        return cb[idx].reshape(idx.shape[0], -1)
    HOST_SHARED["wgT2"] = t(deq(full["L0_ffn_idx_gate"]))
    HOST_SHARED["wuT2"] = t(deq(full["L0_ffn_idx_up"]))
    HOST_SHARED["wdT2"] = t(deq(full["L0_ffn_idx_down"]))


def host_inputs_for_core(c, full):
    b, hf = c // 2, c % 2
    x = np.asarray(full["L0_attn_hidden"][b])
    if hf == 0:
        xe = np.concatenate([x[0:257], x[0:1024], x[0:127]], axis=0)
    else:
        xe = np.concatenate([x[767:2048], x[0:127]], axis=0)
    pp = lambda v: np.ascontiguousarray(np.asarray(v).reshape(FC, 128).T)
    d = {
        "x": np.ascontiguousarray(xe),
        "mask": np.full((128, 1), float(hf), np.float32),
        "mxk": pp(full["L0_attn_time_mix_key"]),
        "mxv": pp(full["L0_attn_time_mix_value"]),
        "mxr": pp(full["L0_attn_time_mix_receptance"]),
        "mxg": pp(full["L0_attn_time_mix_gate"]),
        "tdec": pp(np.asarray(full["L0_attn_time_decay"]).reshape(-1)),
        "tdecr": np.ascontiguousarray(np.asarray(full["L0_attn_time_decay"]).reshape(1, -1)),
        "tfaa": pp(np.asarray(full["L0_attn_time_faaaa"]).reshape(-1)),
        "gnw": np.ascontiguousarray(np.asarray(full["L0_attn_gn_weight"]).reshape(1, -1)),
        "gnb": np.ascontiguousarray(np.asarray(full["L0_attn_gn_bias"]).reshape(1, -1)),
    }
    for kk in ("wkT", "wvT", "wrT", "wgT", "woT", "wgT2", "wuT2", "wdT2"):
        d[kk] = HOST_SHARED[kk]
    return d


def kernel(**inputs):
    from concourse.bass_utils import run_bass_kernel_spmd
    full = {k: np.asarray(v) for k, v in inputs.items()}
    prep_shared(full)
    nc = build_program()
    in_maps = [host_inputs_for_core(c, full) for c in range(8)]
    res = run_bass_kernel_spmd(nc, in_maps, core_ids=list(range(8)))
    out = np.empty((4, 2048, 4096), np.float32)
    for c in range(8):
        b, hf = c // 2, c % 2
        out[b, hf * 1024:(hf + 1) * 1024] = res.results[c]["y"]
    HOST_SHARED.clear()
    return out
